# version 14
# baseline (speedup 1.0000x reference)
import contextlib
import math
import numpy as np
import ml_dtypes

import concourse.bass as bass
import concourse.mybir as mybir
from concourse.bass_utils import run_bass_kernel_spmd

F32 = mybir.dt.float32
BF16 = mybir.dt.bfloat16
AF = mybir.ActivationFunctionType
ALU = mybir.AluOpType
AX = mybir.AxisListType.X

DM = 1024
HID = 2048
QKD = 128
GSZ = 256
NB = 32
MAXD = 128
EPS = 1e-5
DC = DM // 128


def _bucket_1d():
    nb = NB // 2
    max_exact = nb // 2
    out = np.zeros(511, dtype=np.int64)
    for k, d in enumerate(range(-255, 256)):
        ret = nb if d < 0 else 0
        a = abs(d)
        if a < max_exact:
            b = a
        else:
            val = np.float32(math.log(np.float32(a) / np.float32(max_exact))
                             ) if False else None
            num = np.log(np.float32(a) / np.float32(max_exact), dtype=np.float32)
            den = np.float32(math.log(MAXD / max_exact))
            b = max_exact + int(np.float32(num / den) * np.float32(nb - max_exact))
            b = min(b, nb - 1)
        out[k] = ret + b
    return out


def _host_consts():
    ident = np.eye(128, dtype=ml_dtypes.bfloat16)
    ones1 = np.ones((1, 128), dtype=ml_dtypes.bfloat16)
    c128 = np.full((128, 1), 128.0, dtype=np.float32)
    id32 = np.eye(32, dtype=np.float32)
    buck = _bucket_1d()
    oh = np.zeros((NB, 512), dtype=np.float32)
    for k in range(511):
        oh[buck[510 - k], k] = 1.0
    oh_perm = np.zeros((NB, 512), dtype=np.float32)
    for c in range(4):
        for p in range(128):
            oh_perm[:, c * 128 + p] = oh[:, 4 * p + c]
    return ident, ones1, c128, oh_perm, id32


def build_program(NT=32, dbg=False, phases=3):
    assert NT % 4 == 0
    NTOK = NT * 128
    NG = NT // 2
    NS = NT // 4
    L = 8
    XB = L

    nc = bass.Bass("TRN2", target_bir_lowering=False)

    x_h = nc.dram_tensor("x", [NTOK, DM], F32, kind="ExternalInput").ap()
    ng_h = nc.dram_tensor("L0_attn_norm_g", [DM], F32, kind="ExternalInput").ap()
    nb_h = nc.dram_tensor("L0_attn_norm_b", [DM], F32, kind="ExternalInput").ap()
    wh_h = nc.dram_tensor("L0_attn_W_h", [DM, 2 * HID], F32, kind="ExternalInput").ap()
    bh_h = nc.dram_tensor("L0_attn_b_h", [2 * HID], F32, kind="ExternalInput").ap()
    wqk_h = nc.dram_tensor("L0_attn_W_qk", [DM, QKD], F32, kind="ExternalInput").ap()
    bqk_h = nc.dram_tensor("L0_attn_b_qk", [QKD], F32, kind="ExternalInput").ap()
    qg_h = nc.dram_tensor("L0_attn_qk_gamma", [4, QKD], F32, kind="ExternalInput").ap()
    qb_h = nc.dram_tensor("L0_attn_qk_beta", [4, QKD], F32, kind="ExternalInput").ap()
    rel_h = nc.dram_tensor("L0_attn_rel_emb", [NB, 1], F32, kind="ExternalInput").ap()
    wo_h = nc.dram_tensor("L0_attn_W_o", [HID, DM], F32, kind="ExternalInput").ap()
    bo_h = nc.dram_tensor("L0_attn_b_o", [DM], F32, kind="ExternalInput").ap()
    w1_h = nc.dram_tensor("L0_ffn_W1", [DM, 4096], F32, kind="ExternalInput").ap()
    b1_h = nc.dram_tensor("L0_ffn_b1", [4096], F32, kind="ExternalInput").ap()
    w2_h = nc.dram_tensor("L0_ffn_W2", [4096, DM], F32, kind="ExternalInput").ap()
    b2_h = nc.dram_tensor("L0_ffn_b2", [DM], F32, kind="ExternalInput").ap()
    ident_h = nc.dram_tensor("c_ident", [128, 128], BF16, kind="ExternalInput").ap()
    ones_h = nc.dram_tensor("c_ones", [1, 128], BF16, kind="ExternalInput").ap()
    c128_h = nc.dram_tensor("c_128", [128, 1], F32, kind="ExternalInput").ap()
    oh_h = nc.dram_tensor("c_oh", [NB, 512], F32, kind="ExternalInput").ap()
    id32_h = nc.dram_tensor("c_id32", [32, 32], F32, kind="ExternalInput").ap()
    out_h = nc.dram_tensor("out", [NTOK, DM], F32, kind="ExternalOutput").ap()

    v_sp = nc.dram_tensor("v_sp", [NTOK, HID], BF16, kind="Internal").ap()
    g_sp = nc.dram_tensor("g_sp", [NTOK, HID], F32, kind="Internal").ap()
    qq_sp = nc.dram_tensor("qq_sp", [NT, 128, 128], BF16, kind="Internal").ap()
    qkk_sp = nc.dram_tensor("qkk_sp", [NT, 128, 128], BF16, kind="Internal").ap()
    lq_sp = nc.dram_tensor("lq_sp", [NT, 128, 128], BF16, kind="Internal").ap()
    x2_sp = nc.dram_tensor("x2_sp", [NTOK, DM], F32, kind="Internal").ap()
    xfm_sp = nc.dram_tensor("xfm_sp", [NT, 128, DM], BF16, kind="Internal").ap()
    rrev_sp = nc.dram_tensor("rrev_sp", [512], F32, kind="Internal").ap()
    w1bf_sp = nc.dram_tensor("w1bf_sp", [DM, 4096], BF16, kind="Internal").ap()
    w2bf_sp = nc.dram_tensor("w2bf_sp", [4096, DM], BF16, kind="Internal").ap()
    brow_sp = nc.dram_tensor("brow_sp", [QKD], F32, kind="Internal").ap()
    if dbg:
        dbg_h = {
            "dbg_bh": nc.dram_tensor("dbg_bh", [1, 2 * HID], BF16, kind="ExternalOutput").ap(),
            "dbg_bqk": nc.dram_tensor("dbg_bqk", [128, 1], F32, kind="ExternalOutput").ap(),
            "dbg_biasT": nc.dram_tensor("dbg_biasT", [128, 512], F32, kind="ExternalOutput").ap(),
            "dbg_linkv": nc.dram_tensor("dbg_linkv", [128, HID], F32, kind="ExternalOutput").ap(),
            "dbg_v0": nc.dram_tensor("dbg_v0", [128, HID], BF16, kind="ExternalOutput").ap(),
            "dbg_g0": nc.dram_tensor("dbg_g0", [128, HID], F32, kind="ExternalOutput").ap(),
            "dbg_x20": nc.dram_tensor("dbg_x20", [128, DM], F32, kind="ExternalOutput").ap(),
            "dbg_qq0": nc.dram_tensor("dbg_qq0", [128, 128], BF16, kind="ExternalOutput").ap(),
            "dbg_wh0": nc.dram_tensor("dbg_wh0", [128, 512], BF16, kind="ExternalOutput").ap(),
            "dbg_vL": nc.dram_tensor("dbg_vL", [128, HID], BF16, kind="ExternalOutput").ap(),
            "dbg_qqL": nc.dram_tensor("dbg_qqL", [128, 128], BF16, kind="ExternalOutput").ap(),
            "dbg_x2L": nc.dram_tensor("dbg_x2L", [128, DM], F32, kind="ExternalOutput").ap(),
            "dbg_sall": nc.dram_tensor("dbg_sall", [128, NT], F32, kind="ExternalOutput").ap(),
            "dbg_sums": nc.dram_tensor("dbg_sums", [128, NT], F32, kind="ExternalOutput").ap(),
            "dbg_sumsq": nc.dram_tensor("dbg_sumsq", [128, NT], F32, kind="ExternalOutput").ap(),
            "dbg_s128": nc.dram_tensor("dbg_s128", [128, 1], F32, kind="ExternalOutput").ap(),
            "dbg_c128": nc.dram_tensor("dbg_c128", [128, 1], F32, kind="ExternalOutput").ap(),
            "dbg_ng": nc.dram_tensor("dbg_ng", [128, DC], F32, kind="ExternalOutput").ap(),
            "dbg_id32": nc.dram_tensor("dbg_id32", [32, 32], F32, kind="ExternalOutput").ap(),
            "dbg_rrev": nc.dram_tensor("dbg_rrev", [512], F32, kind="ExternalOutput").ap(),
        }

    es_out = contextlib.ExitStack()
    sb = {}

    def alloc(stack, name, shape, dtype, side="left"):
        sb[name] = stack.enter_context(nc.sbuf_tensor(name, shape, dtype, side=side))
        return sb[name]

    es_all = es_out.enter_context(contextlib.ExitStack())
    alloc(es_all, "ident", [128, 128], BF16)
    alloc(es_all, "ones1", [1, 128], BF16)
    alloc(es_all, "c128", [128, 1], F32)
    alloc(es_all, "s128", [128, 1], F32)
    alloc(es_all, "bo_bf", [1, DM], BF16)
    alloc(es_all, "b2_bf", [1, DM], BF16)
    alloc(es_all, "b1_fm", [128, 32], F32)
    alloc(es_all, "gs", [128, 4], F32)
    alloc(es_all, "bs", [128, 4], F32)
    alloc(es_all, "bqk_eff", [128, 1], F32)
    for nm in ("sums", "sumsq", "mu", "nmu", "veps", "rcp", "s_all", "nms"):
        alloc(es_all, nm, [128, NT], F32)

    es_p012 = contextlib.ExitStack()
    alloc(es_p012, "wo", [128, 16 * DM], BF16, side="right")
    alloc(es_p012, "biasT", [128, 512], F32, side="right")
    alloc(es_p012, "linkv", [128, HID], F32, side="right")
    alloc(es_p012, "linkv_bf", [128, HID], BF16, side="right")

    es_p01 = contextlib.ExitStack()
    alloc(es_p01, "wh", [128, DC * 2 * HID], BF16)
    alloc(es_p01, "wqk", [128, DC * QKD], BF16)
    alloc(es_p01, "bh_eff_bf", [1, 2 * HID], BF16)

    es_p1a = contextlib.ExitStack()
    alloc(es_p1a, "xt", [128, XB * DM], F32)
    alloc(es_p1a, "sqscr", [128, DM], BF16)
    alloc(es_p1a, "bh_bf", [1, 2 * HID], BF16)
    alloc(es_p1a, "r4", [128, 4], F32)
    alloc(es_p1a, "hank", [128, 512], F32)
    alloc(es_p1a, "nb_bf", [128, DC], BF16)
    alloc(es_p1a, "ng_fm", [128, DC], F32)

    es_p0 = contextlib.ExitStack()
    alloc(es_p0, "oh", [NB, 512], F32)
    alloc(es_p0, "rel", [NB, 1], F32)
    alloc(es_p0, "ng_tm", [DC, 128], F32)
    alloc(es_p0, "nb_tm", [DC, 128], F32)
    alloc(es_p0, "qg_tm", [4, 128], F32)
    alloc(es_p0, "qb_tm", [4, 128], F32)
    alloc(es_p0, "b1_tm", [32, 128], F32)
    alloc(es_p0, "id32", [32, 32], F32)
    alloc(es_p0, "nb_fm", [128, DC], F32)
    alloc(es_p0, "brow_qk", [1, QKD], F32)
    alloc(es_p0, "brow_qk_t", [128, 1], F32)
    alloc(es_p0, "bqk_f32", [128, 1], F32)
    alloc(es_p0, "qg", [128, 4], F32)
    alloc(es_p0, "qb", [128, 4], F32)
    alloc(es_p0, "bo_f32", [1, DM], F32)
    alloc(es_p0, "b2_f32", [1, DM], F32)
    es_p0.close()

    es_p1 = contextlib.ExitStack()
    alloc(es_p1, "n0", [128, 3 * DM], BF16)
    alloc(es_p1, "nfm", [128, 2 * DM], BF16)
    alloc(es_p1, "qkf", [128, 2 * QKD], F32)
    alloc(es_p1, "lkfm", [128, 2 * QKD], BF16)
    alloc(es_p1, "lktm", [128, 2 * QKD], BF16)
    alloc(es_p1, "vt", [128, 2 * HID], BF16)
    alloc(es_p1, "gt", [128, 4 * DM], F32)
    alloc(es_p1, "qq_st", [128, 2 * 128], BF16)
    alloc(es_p1, "qkk_st", [128, 2 * 128], BF16)
    alloc(es_p1, "lq_st", [128, 2 * 128], BF16)
    es_p1.close()
    es_p1a.close()
    es_p01.close()

    es_p23 = es_out.enter_context(contextlib.ExitStack())
    alloc(es_p23, "w1", [128, DC * 4096], BF16)

    es_p2 = contextlib.ExitStack()
    alloc(es_p2, "qq2", [128, 3 * 256], BF16)
    alloc(es_p2, "qkk2", [128, 3 * 256], BF16)
    alloc(es_p2, "lq2", [128, 3 * 128], BF16)
    alloc(es_p2, "v2", [128, 3 * HID], BF16)
    alloc(es_p2, "g2", [128, 3 * DM], F32)
    alloc(es_p2, "x2t", [128, 2 * DM], F32)
    alloc(es_p2, "pre2", [128, 2 * DM], F32)
    alloc(es_p2, "sims", [128, 2 * 512], F32)
    alloc(es_p2, "attnT", [128, 2 * 512], BF16)
    alloc(es_p2, "gtm", [128, 2 * HID], BF16)
    alloc(es_p2, "gfm", [128, 2 * HID], BF16)
    alloc(es_p2, "x2bf", [128, 2 * DM], BF16)
    alloc(es_p2, "x2f", [128, 2 * DM], F32)
    alloc(es_p2, "xfmt", [128, 2 * DM], BF16)
    alloc(es_p2, "sq2", [128, DM], BF16)
    es_p2.close()
    es_p012.close()

    es_p3 = es_out.enter_context(contextlib.ExitStack())
    alloc(es_p3, "w2", [128, 32 * DM], BF16)
    alloc(es_p3, "xfm3", [128, 2 * 4 * DM], BF16)
    alloc(es_p3, "h1", [128, 32 * 512], BF16)
    alloc(es_p3, "x2r", [128, 2 * DM], F32)
    alloc(es_p3, "pre3", [128, 2 * DM], F32)
    alloc(es_p3, "outst", [128, 2 * DM], F32)
    alloc(es_p3, "sq3", [128, DM], BF16)

    E0 = es_out.enter_context(nc.psum_tensor("E0", [128, 2048], F32))
    E1 = es_out.enter_context(nc.psum_tensor("E1", [128, 2048], F32))

    semnames = [
        "sQK", "sWo", "sW1", "sW2", "sPar", "sPar2", "sRel", "sPC1", "sPC2",
        "c0t", "c0v", "c0s",
        "c_xfv", "c_sqst", "c_v1", "c_sq1", "c_sv", "c_n0",
        "c_T", "c_Tev", "c_qk", "c_qkev", "c_heads", "c_lkT", "c_lktm",
        "c_hvmm", "c_hvev", "c_lkvmm", "c_lkvadd",
        "c_lkvbf",
        "c_sim", "c_simb", "c_relu", "c_attnT", "c_qlin", "c_gated",
        "c_gtmm", "c_gtev", "c_womm", "c_pre2", "c_v2a", "c_s2a", "c_v2b",
        "c_x2bf", "c_x2f", "c_x2T", "c_xfmev",
        "c_w1mm", "c_h1ev", "c_w2mm", "c_pre3", "c_p2h",
        "c_v3a", "c_s3a", "c_v3b", "c_ostg", "vq",
    ]
    S = {}
    for nm in semnames:
        S[nm] = es_out.enter_context(nc.semaphore(nm))
    swh = [es_out.enter_context(nc.semaphore(f"swh{g}")) for g in range(8)]

    class DS:

        def __init__(self, name):
            self.s = [es_out.enter_context(nc.semaphore(name + "a")),
                      es_out.enter_context(nc.semaphore(name + "b"))]
            self.n = 0

        def issue(self, eng, out, in_, **kw):
            i = self.n
            if i >= 2:
                eng.wait_ge(self.s[i % 2], 16 * (i // 2))
            eng.dma_start(out=out, in_=in_, **kw).then_inc(self.s[i % 2], 16)
            self.n += 1
            return i

        def wait(self, eng, i):
            if i >= 0:
                eng.wait_ge(self.s[i % 2], 16 * (i // 2 + 1))

        def wait_upto(self, eng, i):
            self.wait(eng, i)
            self.wait(eng, i - 1)

    D = {nm: DS(nm) for nm in (
        "x1", "vout", "gout", "hsp", "qq2", "qk2", "lq2", "v2", "g2",
        "x2", "x2o", "xfmo", "xfm", "x2r", "outs")}

    def wge(eng, nm, v):
        if v > 0:
            eng.wait_ge(S[nm], v)

    T = sb

    def ap2(t, off, dims):
        return bass.AP(t[:].tensor, off, dims)

    C0 = {"t": 0, "v": 0, "s": 0}
    NPAR = 12

    def emit_sync(sync):
        sync.dma_start(out=T["ng_tm"][:], in_=ap2(ng_h, 0, [[128, DC], [1, 128]])).then_inc(S["sPar"], 16)
        sync.dma_start(out=T["nb_tm"][:], in_=ap2(nb_h, 0, [[128, DC], [1, 128]])).then_inc(S["sPar"], 16)
        sync.dma_start(out=T["bqk_f32"][:], in_=ap2(bqk_h, 0, [[1, 128], [1, 1]])).then_inc(S["sPar"], 16)
        sync.dma_start(out=T["qg_tm"][:], in_=qg_h[:, :]).then_inc(S["sPar"], 16)
        sync.wait_ge(S["sPar"], 64)
        sync.dma_start(out=T["qb_tm"][:], in_=qb_h[:, :]).then_inc(S["sPar"], 16)
        sync.dma_start(out=T["rel"][:], in_=rel_h[:, :]).then_inc(S["sPar"], 16)
        sync.dma_start(out=T["oh"][:], in_=oh_h[:, :]).then_inc(S["sPar"], 16)
        sync.dma_start(out=T["ident"][:], in_=ident_h[:, :]).then_inc(S["sPar"], 16)
        sync.wait_ge(S["sPar"], 128)
        sync.dma_start(out=T["ones1"][:], in_=ones_h[:, :]).then_inc(S["sPar"], 16)
        sync.dma_start(out=T["c128"][:], in_=c128_h[:, :]).then_inc(S["sPar"], 16)
        sync.dma_start(out=T["id32"][:], in_=id32_h[:, :]).then_inc(S["sPar"], 16)
        sync.dma_start(out=T["b1_tm"][:], in_=ap2(b1_h, 0, [[128, 32], [1, 128]])).then_inc(S["sPar"], 16)
        sync.dma_start(out=T["bo_f32"][:], in_=ap2(bo_h, 0, [[DM, 1], [1, DM]])).then_inc(S["sPar2"], 16)
        sync.dma_start(out=T["b2_f32"][:], in_=ap2(b2_h, 0, [[DM, 1], [1, DM]])).then_inc(S["sPar2"], 16)
        sync.wait_ge(S["sPar2"], 32)
        if phases >= 1:
            for t in range(min(6, XB, NT)):
                D["x1"].issue(sync, T["xt"][:, (t % XB) * DM:(t % XB + 1) * DM],
                              x_h[t * 128:(t + 1) * 128, :])
        wge(sync, "c0v", BROWQK_CV[0])
        sync.dma_start(out=brow_sp[:], in_=T["brow_qk"][0:1, :]).then_inc(S["sPar2"], 16)
        sync.wait_ge(S["sPar2"], 48)
        sync.dma_start(out=T["brow_qk_t"][:], in_=ap2(brow_sp, 0, [[1, 128], [1, 1]])).then_inc(S["sPar2"], 16)
        wge(sync, "c0v", R4_CV[0])
        sync.dma_start(out=rrev_sp.rearrange("(p c) -> p c", c=4),
                       in_=T["r4"][:]).then_inc(S["sRel"], 16)
        sync.wait_ge(S["sRel"], 16)
        sync.dma_start(out=T["hank"][:], in_=ap2(rrev_sp, 0, [[1, 128], [128, 2], [1, 256]])).then_inc(S["sRel"], 16)

        if phases < 1:
            wge(sync, "c0v", CV_FOLDS_DONE[0])
            wge(sync, "c0t", 3)
            wge(sync, "c0s", 1)
            _dbg_dump(sync)
            return
        for t in range(min(6, XB, NT), min(XB, NT)):
            D["x1"].issue(sync, T["xt"][:, (t % XB) * DM:(t % XB + 1) * DM],
                          x_h[t * 128:(t + 1) * 128, :])
        for t in range(NT):
            u = t + XB
            if u < NT:
                wge(sync, "c_n0", u - XB + 1)
                wge(sync, "c_xfv", u - XB + 1)
                D["x1"].issue(sync, T["xt"][:, (u % XB) * DM:(u % XB + 1) * DM],
                              x_h[u * 128:(u + 1) * 128, :])
            wge(sync, "c_hvev", 8 * t + 4)
            D["vout"].issue(sync, v_sp[t * 128:(t + 1) * 128, :],
                            T["vt"][:, (t % 2) * HID:(t % 2 + 1) * HID])

        if phases < 2:
            D["vout"].wait_upto(sync, NT - 1)
            D["gout"].wait_upto(sync, 2 * NT - 1)
            D["hsp"].wait_upto(sync, 3 * NT - 1)
            _dbg_dump(sync)
            return
        D["vout"].wait_upto(sync, NT - 1)
        D["gout"].wait_upto(sync, 2 * NT - 1)
        D["hsp"].wait_upto(sync, 3 * NT - 1)
        wge(sync, "c_hvmm", 8 * NT)
        wge(sync, "c_lkvmm", NT)
        wge(sync, "c_hvev", 8 * NT)
        wge(sync, "c_lkvadd", NT)

        def load_grp(g):
            D["qq2"].issue(sync, T["qq2"][:, (g % 3) * 256:(g % 3 + 1) * 256],
                           ap2(qq_sp, 2 * g * 128 * 128, [[128, 128], [128 * 128, 2], [1, 128]]))
            D["qk2"].issue(sync, T["qkk2"][:, (g % 3) * 256:(g % 3 + 1) * 256],
                           ap2(qkk_sp, 2 * g * 128 * 128, [[128, 128], [128 * 128, 2], [1, 128]]))

        def load_lq(u):
            D["lq2"].issue(sync, T["lq2"][:, (u % 3) * 128:(u % 3 + 1) * 128],
                           lq_sp[u, :, :])

        load_grp(0)
        if NT >= 1:
            load_lq(0)
            D["v2"].issue(sync, T["v2"][:, 0:HID], v_sp[0:128, :])
        if NT >= 2:
            D["v2"].issue(sync, T["v2"][:, HID:2 * HID], v_sp[128:256, :])
            D["x2"].issue(sync, T["x2t"][:, 0:DM], x_h[0:128, :])
        for g in range(1, min(3, NG)):
            load_grp(g)
        for u in range(1, min(3, NT)):
            load_lq(u)
        if NT >= 3:
            D["v2"].issue(sync, T["v2"][:, 2 * HID:3 * HID], v_sp[256:384, :])
        if NT >= 2:
            D["x2"].issue(sync, T["x2t"][:, DM:2 * DM], x_h[128:256, :])
        sync.wait_ge(S["sPC1"], 16 * 32)
        wge(sync, "c_hvmm", 8 * NT)
        wge(sync, "c_lkvmm", NT)
        for c in range(DC):
            if c == 4:
                sync.wait_ge(S["sW1"], 64)
            sync.dma_start(
                out=bass.AP(T["w1"][:].tensor, c * 4096, [[DC * 4096, 128], [1, 4096]]),
                in_=ap2(w1bf_sp, c * 128 * 4096, [[4096, 128], [1, 4096]]),
            ).then_inc(S["sW1"], 16)
        for tau in range(NT):
            g = tau // 2
            if tau % 2 == 0 and g + 3 < NG:
                wge(sync, "c_sim", g + 1)
                load_grp(g + 3)
            if tau + 3 < NT:
                wge(sync, "c_qlin", 4 * tau + 4)
                load_lq(tau + 3)
            u = tau + 3
            if u < NT:
                prev = u - 3
                last_rd = 2 * (prev // 2) + 1
                wge(sync, "c_qlin", 4 * (last_rd + 1))
                D["v2"].issue(sync, T["v2"][:, (u % 3) * HID:(u % 3 + 1) * HID],
                              v_sp[u * 128:(u + 1) * 128, :])
            u2 = tau + 2
            if u2 < NT:
                wge(sync, "c_pre2", u2 - 1)
                D["x2"].issue(sync, T["x2t"][:, (u2 % 2) * DM:(u2 % 2 + 1) * DM],
                              x_h[u2 * 128:(u2 + 1) * 128, :])

        if phases < 3:
            D["x2o"].wait_upto(sync, NT - 1)
            D["xfmo"].wait_upto(sync, NT - 1)
            _dbg_dump(sync)
            return
        D["x2o"].wait_upto(sync, NT - 1)
        D["xfmo"].wait_upto(sync, NT - 1)
        wge(sync, "c_x2T", 2 * NT)
        wge(sync, "c_xfmev", 2 * NT)
        wge(sync, "c_x2f", NT)
        wge(sync, "c_x2bf", NT)
        def load_xfm(Ti):
            for tt in range(4):
                t = 4 * Ti + tt
                D["xfm"].issue(
                    sync,
                    bass.AP(T["xfm3"][:].tensor, (Ti % 2) * 4 * DM + tt * 128,
                            [[2 * 4 * DM, 128], [512, DC], [1, 128]]),
                    xfm_sp[t, :, :])

        load_xfm(0)
        if NS > 1:
            load_xfm(1)
        sync.wait_ge(S["sPC2"], 16 * 32)
        wge(sync, "c_x2bf", NT)
        D["g2"].wait_upto(sync, 2 * NT - 1)
        D["v2"].wait_upto(sync, NT - 1)
        D["x2"].wait_upto(sync, NT - 1)
        D["qq2"].wait_upto(sync, NG - 1)
        D["qk2"].wait_upto(sync, NG - 1)
        D["lq2"].wait_upto(sync, NT - 1)
        for g in range(4):
            if g >= 1:
                sync.wait_ge(S["sW2"], 16 * g)
            sync.dma_start(
                out=bass.AP(T["w2"][:].tensor, g * 8 * DM, [[32 * DM, 128], [DM, 8], [1, DM]]),
                in_=ap2(w2bf_sp, g * 8 * 128 * DM, [[DM, 128], [128 * DM, 8], [1, DM]]),
            ).then_inc(S["sW2"], 16)
        for Ti in range(NS):
            if Ti + 2 < NS:
                wge(sync, "c_w1mm", 32 * (Ti + 1))
                load_xfm(Ti + 2)
            for tt in range(4):
                t = 4 * Ti + tt
                wge(sync, "c_pre3", t - 1)
                D["x2r"].issue(sync, T["x2r"][:, (t % 2) * DM:(t % 2 + 1) * DM],
                               x2_sp[t * 128:(t + 1) * 128, :])
            for tt in range(4):
                t = 4 * Ti + tt
                wge(sync, "c_ostg", t + 1)
                D["outs"].issue(sync, out_h[t * 128:(t + 1) * 128, :],
                                T["outst"][:, (t % 2) * DM:(t % 2 + 1) * DM])
        D["outs"].wait_upto(sync, NT - 1)
        _dbg_dump(sync)

    def _dbg_dump(sync):
        if not dbg:
            return
        dsem = S["sRel"]
        dv = [32]
        def ddump(out_ap, in_ap):
            sync.dma_start(out=out_ap, in_=in_ap).then_inc(dsem, 16)
            dv[0] += 16
            sync.wait_ge(dsem, dv[0])
        ddump(dbg_h["dbg_bh"], T["bh_eff_bf"][0:1, :])
        ddump(dbg_h["dbg_bqk"], T["bqk_eff"][:])
        ddump(dbg_h["dbg_biasT"], T["biasT"][:])
        ddump(dbg_h["dbg_linkv"], T["linkv"][:])
        ddump(dbg_h["dbg_v0"], v_sp[0:128, :])
        ddump(dbg_h["dbg_g0"], g_sp[0:128, :])
        ddump(dbg_h["dbg_x20"], x2_sp[0:128, :])
        ddump(dbg_h["dbg_qq0"], qq_sp[0, :, :])
        ddump(dbg_h["dbg_wh0"], T["wh"][:, 0:512])
        ddump(dbg_h["dbg_vL"], v_sp[(NT - 1) * 128:NT * 128, :])
        ddump(dbg_h["dbg_qqL"], qq_sp[NT - 1, :, :])
        ddump(dbg_h["dbg_x2L"], x2_sp[(NT - 1) * 128:NT * 128, :])
        ddump(dbg_h["dbg_sall"], T["s_all"][:])
        ddump(dbg_h["dbg_sums"], T["sums"][:])
        ddump(dbg_h["dbg_sumsq"], T["sumsq"][:])
        ddump(dbg_h["dbg_s128"], T["s128"][:])
        ddump(dbg_h["dbg_c128"], T["c128"][:])
        ddump(dbg_h["dbg_ng"], T["ng_fm"][:])
        ddump(dbg_h["dbg_id32"], T["id32"][:])
        ddump(dbg_h["dbg_rrev"], rrev_sp[:])

    def emit_gpsimd(gp):
        gp.dma_start(out=T["bh_bf"][:], in_=ap2(bh_h, 0, [[4096, 1], [1, 2 * HID]])).then_inc(S["sQK"], 16)
        gp.wait_ge(S["sQK"], 16)
        gp.dma_start(out=T["wqk"][:].rearrange("p (c q) -> p c q", c=DC),
                     in_=ap2(wqk_h, 0, [[QKD, 128], [128 * QKD, DC], [1, QKD]])).then_inc(S["sQK"], 16)
        for g in range(8):
            gp.dma_start(
                out=bass.AP(T["wh"][:].tensor, g * 512, [[DC * 2 * HID, 128], [2 * HID, DC], [1, 512]]),
                in_=ap2(wh_h, g * 512, [[2 * HID, 128], [128 * 2 * HID, DC], [1, 512]]),
            ).then_inc(swh[g], 16)
        for g in range(4):
            gp.dma_start(
                out=bass.AP(T["wo"][:].tensor, g * 4 * DM, [[16 * DM, 128], [DM, 4], [1, DM]]),
                in_=ap2(wo_h, g * 4 * 128 * DM, [[DM, 128], [128 * DM, 4], [1, DM]]),
            ).then_inc(S["sWo"], 16)
        if phases < 1:
            return
        npc = [0, 0]

        def precast(k):
            if k < 32:
                i = npc[0]; npc[0] += 1
                if i >= 4 and i % 4 == 0:
                    gp.wait_ge(S["sPC1"], 16 * i)
                gp.dma_start(out=ap2(w1bf_sp, k * 32 * 4096, [[4096, 32], [1, 4096]]),
                             in_=ap2(w1_h, k * 32 * 4096, [[4096, 32], [1, 4096]])
                             ).then_inc(S["sPC1"], 16)
            else:
                c = k - 32
                i = npc[1]; npc[1] += 1
                if i >= 4 and i % 4 == 0:
                    gp.wait_ge(S["sPC2"], 16 * i)
                gp.dma_start(out=ap2(w2bf_sp, c * 128 * DM, [[DM, 128], [1, DM]]),
                             in_=ap2(w2_h, c * 128 * DM, [[DM, 128], [1, DM]])
                             ).then_inc(S["sPC2"], 16)

        for t in range(NT):
            wge(gp, "c_heads", t + 1)
            sl = (t % 2) * 128
            D["hsp"].issue(gp, qq_sp[t, :, :], T["qq_st"][:, sl:sl + 128])
            D["hsp"].issue(gp, qkk_sp[t, :, :], T["qkk_st"][:, sl:sl + 128])
            D["hsp"].issue(gp, lq_sp[t, :, :], T["lq_st"][:, sl:sl + 128])
            NPT = max(NT - 4, 1)
            for k in range(min(t, NPT) * 64 // NPT, min(t + 1, NPT) * 64 // NPT):
                precast(k)

    def emit_tensor(tn):
        ID = T["ident"][:]
        tn.wait_ge(S["sPar"], 16 * NPAR)
        specs = [("ng_tm", DC, 1536), ("nb_tm", DC, 1544), ("qg_tm", 4, 1552),
                 ("qb_tm", 4, 1556), ("b1_tm", 32, 1560)]
        for i, (nm, kdim, off) in enumerate(specs):
            mm = tn.matmul(E0[:, off:off + kdim], T[nm][:, :],
                           T["id32"][0:kdim, 0:kdim],
                           start=(i == 0), stop=(i == len(specs) - 1),
                           skip_group_check=True)
        mm.then_inc(S["c0t"], 1)
        for c in range(4):
            mm = tn.matmul(E0[:, 1024 + c:1024 + c + 1],
                      T["oh"][:, c * 128:(c + 1) * 128],
                      T["rel"][:, :],
                      start=(c == 0), stop=(c == 3), skip_group_check=True)
        mm.then_inc(S["c0t"], 1)
        tn.wait_ge(S["sQK"], 32)
        wge(tn, "c0v", CVD["nbbf"])
        for c in range(DC):
            mm = tn.matmul(E0[0:1, 512:512 + QKD],
                      T["nb_bf"][:, c:c + 1],
                      T["wqk"][:, c * QKD:(c + 1) * QKD],
                      start=(c == 0), stop=(c == DC - 1))
        mm.then_inc(S["c0t"], 1)

        if phases < 1:
            return

        def p0_brow(s):
            tn.wait_ge(swh[s], 16)
            if s == 1:
                wge(tn, "c0v", BROWQK_CV[0])
            if s >= 1:
                wge(tn, "c0v", BHADD_CV[s - 1])
            wge(tn, "c_hvev", s - 1)
            pslc = E0[0:1, (s % 2) * 512:(s % 2) * 512 + 512]
            for c in range(DC):
                mm = tn.matmul(pslc,
                          T["nb_bf"][:, c:c + 1],
                          T["wh"][:, c * 2 * HID + s * 512:c * 2 * HID + (s + 1) * 512],
                          start=(c == 0), stop=(c == DC - 1))
            mm.then_inc(S["c0t"], 1)

        def p1_T(u):
            if not (0 <= u < NT):
                return
            sl3 = (u % 3) * DM
            if u == 0:
                wge(tn, "c0v", R4_CV[0])
            wge(tn, "c_n0", u + 1)
            wge(tn, "c_Tev", 2 * u - 1)
            for c in range(4):
                mm = tn.matmul(E0[:, 1024 + c * 128:1024 + (c + 1) * 128],
                               T["n0"][:, sl3 + c * 128:sl3 + (c + 1) * 128],
                               ID, start=(c == 0), stop=(c == 3), skip_group_check=True)
            mm.then_inc(S["c_T"], 1)
            wge(tn, "c_Tev", 2 * u)
            for c in range(4):
                mm = tn.matmul(E0[:, 1536 + c * 128:1536 + (c + 1) * 128],
                               T["n0"][:, sl3 + (4 + c) * 128:sl3 + (5 + c) * 128],
                               ID, start=(c == 0), stop=(c == 3), skip_group_check=True)
            mm.then_inc(S["c_T"], 1)

        def p1_qk(u):
            if not (0 <= u < NT):
                return
            sl2 = (u % 2) * DM
            if u == 0:
                wge(tn, "c0v", WQK_CV[0])
            wge(tn, "c_Tev", 2 * u + 2)
            wge(tn, "c_lkvadd", u - 1)
            for c in range(DC):
                mm = tn.matmul(E1[:, 0:QKD],
                               T["wqk"][:, c * QKD:(c + 1) * QKD],
                               T["nfm"][:, sl2 + c * 128:sl2 + (c + 1) * 128],
                               start=(c == 0), stop=(c == DC - 1))
            mm.then_inc(S["c_qk"], 1)

        def p1_hv(t, s):
            sl2 = (t % 2) * DM
            if t == 0:
                wge(tn, "c0s", SFOLD_CV[s])
                wge(tn, "c0v", BHADD_CV[s])
            wge(tn, "c_hvev", 8 * t + s - 1)
            if t == 0 and s == 0:
                wge(tn, "c_Tev", 2)
            pslc = E0[:, (s % 2) * 512:(s % 2) * 512 + 512]
            tn.matmul(pslc, T["ones1"][:], T["bh_eff_bf"][0:1, s * 512:(s + 1) * 512],
                      start=True, stop=False)
            for c in range(DC):
                mm = tn.matmul(pslc,
                               T["nfm"][:, sl2 + c * 128:sl2 + (c + 1) * 128],
                               T["wh"][:, c * 2 * HID + s * 512:c * 2 * HID + (s + 1) * 512],
                               start=False, stop=(c == DC - 1))
            mm.then_inc(S["c_hvmm"], 1)

        def p1_linkT(t):
            wge(tn, "c_heads", t + 1)
            wge(tn, "c_lkvadd", t)
            tn.matmul(E1[:, 512:512 + QKD],
                      T["lkfm"][:, (t % 2) * 128:(t % 2 + 1) * 128], ID,
                      start=True, stop=True).then_inc(S["c_lkT"], 1)

        def p1_linkv(t):
            wge(tn, "c_lktm", t + 1)
            wge(tn, "c_hvev", 8 * t + 4)
            wge(tn, "c_lkvadd", t)
            wge(tn, "c_qkev", min(t + 2, NT))
            for b in range(4):
                mm = tn.matmul(E1[:, b * 512:(b + 1) * 512],
                               T["lktm"][:, (t % 2) * 128:(t % 2 + 1) * 128],
                               T["vt"][:, (t % 2) * HID + b * 512:(t % 2) * HID + (b + 1) * 512],
                               start=True, stop=True)
            mm.then_inc(S["c_lkvmm"], 1)

        p1_T(0)
        p1_qk(0)
        for t in range(NT):
            p1_T(t + 1)
            if t == 0:
                for s in range(8):
                    p0_brow(s)
                    p1_hv(0, s)
                    if s == 1:
                        p1_qk(1)
                    elif s == 2:
                        p1_linkT(0)
                    elif s == 3:
                        p1_linkv(0)
                continue
            p1_hv(t, 0)
            p1_hv(t, 1)
            p1_qk(t + 1)
            p1_hv(t, 2)
            p1_hv(t, 3)
            p1_linkT(t)
            p1_hv(t, 4)
            p1_hv(t, 5)
            p1_linkv(t)
            p1_hv(t, 6)
            p1_hv(t, 7)

        if phases < 2:
            return
        wge(tn, "c_hvev", 8 * NT)
        wge(tn, "c_lkvadd", NT)
        tn.wait_ge(S["sWo"], 64)
        wge(tn, "c_lkvbf", 1)

        def p2_sim(gg, tau):
            if not (0 <= gg < NG):
                return
            D["qq2"].wait(tn, gg)
            D["qk2"].wait(tn, gg)
            wge(tn, "c_pre2", tau - 1)
            wge(tn, "c_attnT", gg - 1)
            for j in range(2):
                mm = tn.matmul(E0[:, j * 256:(j + 1) * 256],
                          T["qkk2"][:, (gg % 3) * 256 + j * 128:(gg % 3) * 256 + (j + 1) * 128],
                          T["qq2"][:, (gg % 3) * 256:(gg % 3 + 1) * 256],
                          start=(j == 0), stop=(j == 1), skip_group_check=True)
            mm.then_inc(S["c_sim"], 1)

        def p2_quad(tau):
            g = tau // 2
            wge(tn, "c_attnT", g + 1)
            D["v2"].wait_upto(tn, 2 * g + 1)
            D["lq2"].wait(tn, tau)
            for b in range(4):
                wge(tn, "c_gated", 4 * (tau - 1) + b + 1)
                ps = E1[:, b * 512:(b + 1) * 512]
                for j in range(2):
                    vtile = 2 * g + j
                    a0 = (g % 2) * 512 + j * 256 + (tau % 2) * 128
                    tn.matmul(ps,
                              T["attnT"][:, a0:a0 + 128],
                              T["v2"][:, (vtile % 3) * HID + b * 512:(vtile % 3) * HID + (b + 1) * 512],
                              start=(j == 0), stop=False)
                tn.matmul(ps, T["lq2"][:, (tau % 3) * 128:(tau % 3 + 1) * 128],
                          T["linkv_bf"][:, b * 512:(b + 1) * 512],
                          start=False, stop=True).then_inc(S["c_qlin"], 1)

        def p2_gatedT(u):
            wge(tn, "c_gated", 4 * u + 4)
            wge(tn, "c_gtev", 4 * (u - 1) + 4)
            wge(tn, "c_xfmev", 2 * (u - 2) + 2)
            for b in range(4):
                if b >= 2:
                    wge(tn, "c_gtev", 4 * u + b - 1)
                bank = 1024 if b % 2 == 0 else 512
                for k in range(4):
                    ch = b * 4 + k
                    mm = tn.matmul(E0[:, bank + k * 128:bank + (k + 1) * 128],
                              T["gtm"][:, (u % 2) * HID + ch * 128:(u % 2) * HID + (ch + 1) * 128],
                              ID, start=(k == 0), stop=(k == 3), skip_group_check=True)
                mm.then_inc(S["c_gtmm"], 1)

        def p2_Wo(u, tau):
            g = u // 2
            wge(tn, "c_xfmev", max(2 * (u - 2) + 1, 0))
            wge(tn, "c_pre2", u)
            for h in range(2):
                if h == 1:
                    wge(tn, "c_gtev", 4 * u + 4)
                    if tau % 2 == 0 and tau // 2 + 1 < NG:
                        wge(tn, "c_simb", tau // 2 + 2)
                    wge(tn, "c_pre2", tau - 1)
                ps = E0[:, 1536:2048] if h == 0 else E0[:, 0:512]
                tn.matmul(ps, T["ones1"][:], T["bo_bf"][0:1, h * 512:(h + 1) * 512],
                          start=True, stop=False)
                for k in range(16):
                    if h == 0 and k % 4 == 0:
                        wge(tn, "c_gtev", 4 * u + k // 4 + 1)
                    mm = tn.matmul(ps, T["gfm"][:, (u % 2) * HID + k * 128:(u % 2) * HID + (k + 1) * 128],
                              T["wo"][:, k * DM + h * 512:k * DM + (h + 1) * 512],
                              start=False, stop=(k == 15))
                mm.then_inc(S["c_womm"], 1)

        p2_sim(0, -1)
        for tau in range(NT):
            _emit_x2T(tn, tau - 3)
            p2_quad(tau)
            if tau % 2 == 0:
                p2_sim(tau // 2 + 1, tau)
            if tau >= 1:
                p2_gatedT(tau - 1)
                p2_Wo(tau - 1, tau)
        _emit_x2T(tn, NT - 3)
        p2_gatedT(NT - 1)
        p2_Wo(NT - 1, NT)
        _emit_x2T(tn, NT - 2)
        _emit_x2T(tn, NT - 1)

        if phases < 3:
            return
        wge(tn, "c_xfmev", 2 * NT)
        wge(tn, "c_x2bf", NT)
        tn.wait_ge(S["sW1"], 16 * DC)
        for Ti in range(NS):
            D["xfm"].wait_upto(tn, 4 * Ti + 3)
            for ci in range(32):
                if ci % 4 == 0:
                    wge(tn, "c_h1ev", 32 * Ti + ci)
                ps = E0[:, (ci % 4) * 512:(ci % 4 + 1) * 512]
                for dc in range(DC):
                    mm = tn.matmul(ps,
                              T["w1"][:, dc * 4096 + ci * 128:dc * 4096 + (ci + 1) * 128],
                              T["xfm3"][:, (Ti % 2) * 4 * DM + dc * 512:(Ti % 2) * 4 * DM + (dc + 1) * 512],
                              start=(dc == 0), stop=(dc == DC - 1))
                mm.then_inc(S["c_w1mm"], 1)
            wge(tn, "c_h1ev", 32 * (Ti + 1))
            for tt in range(4):
                t = 4 * Ti + tt
                wge(tn, "c_pre3", t - 1)
                base = (tt % 2) * 1024
                for h in range(2):
                    ps = E1[:, base + h * 512:base + (h + 1) * 512]
                    tn.matmul(ps, T["ones1"][:], T["b2_bf"][0:1, h * 512:(h + 1) * 512],
                              start=True, stop=False)
                    for ci in range(32):
                        if Ti == 0 and tt == 0 and h == 0 and ci % 8 == 0:
                            tn.wait_ge(S["sW2"], 16 * (ci // 8 + 1))
                        mm = tn.matmul(ps,
                                  T["h1"][:, ci * 512 + tt * 128:ci * 512 + (tt + 1) * 128],
                                  T["w2"][:, ci * DM + h * 512:ci * DM + (h + 1) * 512],
                                  start=False, stop=(ci == 31))
                mm.then_inc(S["c_w2mm"], 1)

    def _emit_x2T(tn, tau):
        if tau < 0:
            return
        wge(tn, "c_x2bf", tau + 1)
        wge(tn, "c_gtev", min(4 * (tau + 1) + 4, 4 * NT))
        wge(tn, "c_p2h", min(tau + 2, NT))
        wge(tn, "c_xfmev", 2 * (tau - 1) + 2)
        for b in range(2):
            bank = 1536 if b == 0 else 512
            for k in range(4):
                c = b * 4 + k
                mm = tn.matmul(E0[:, bank + k * 128:bank + (k + 1) * 128],
                          T["x2bf"][:, (tau % 2) * DM + c * 128:(tau % 2) * DM + (c + 1) * 128],
                          T["ident"][:], start=(k == 0), stop=(k == 3), skip_group_check=True)
            mm.then_inc(S["c_x2T"], 1)

    def emit_scalar(sc):
        sc.wait_ge(S["sPar"], 16 * NPAR)
        sc.activation(T["s128"][:], T["c128"][:], AF.Sqrt).then_inc(S["c0s"], 1)
        C0["s"] += 1
        if phases < 1:
            wge(sc, "c0v", CV_FOLDS_DONE[0])
            return

        def stats_sq(u):
            D["x1"].wait(sc, u)
            wge(sc, "c_sqst", u)
            sc.activation(T["sqscr"][:],
                          T["xt"][:, (u % XB) * DM:(u % XB + 1) * DM],
                          AF.Square, accum_out=T["sumsq"][:, u:u + 1]).then_inc(S["c_sqst"], 1)

        def sqrt_batch(g):
            wge(sc, "c_v1", g + 1)
            sc.activation(T["s_all"][:, 4 * g:4 * g + 4], T["rcp"][:, 4 * g:4 * g + 4],
                          AF.Sqrt).then_inc(S["c_sq1"], 1)

        def p1_n0(u):
            if not (0 <= u < NT):
                return
            wge(sc, "c_sv", u // 4 + 1)
            wge(sc, "c_T", 2 * u - 4)
            sc.activation(T["n0"][:, (u % 3) * DM:(u % 3 + 1) * DM],
                          T["xt"][:, (u % XB) * DM:(u % XB + 1) * DM],
                          AF.Identity, bias=T["nms"][:, u:u + 1],
                          scale=T["s_all"][:, u:u + 1]).then_inc(S["c_n0"], 1)

        def p1_ev(u):
            if not (0 <= u < NT):
                return
            wge(sc, "c_T", 2 * u + 1)
            wge(sc, "c_hvmm", 8 * u - 8)
            sc.copy(T["nfm"][:, (u % 2) * DM:(u % 2) * DM + 512], E0[:, 1024:1536]).then_inc(S["c_Tev"], 1)
            wge(sc, "c_T", 2 * u + 2)
            sc.copy(T["nfm"][:, (u % 2) * DM + 512:(u % 2) * DM + 1024], E0[:, 1536:2048]).then_inc(S["c_Tev"], 1)

        def p1_qkev(u):
            if not (0 <= u < NT):
                return
            wge(sc, "c_qk", u + 1)
            wge(sc, "c_heads", u - 1)
            sc.activation(T["qkf"][:, (u % 2) * QKD:(u % 2 + 1) * QKD],
                          E1[:, 0:QKD], AF.Silu,
                          bias=T["bqk_eff"][:]).then_inc(S["c_qkev"], 1)

        def p1_hvev(t, s):
            wge(sc, "c_hvmm", 8 * t + s + 1)
            src_ = E0[:, (s % 2) * 512:(s % 2) * 512 + 512]
            if s < 4:
                if s == 0:
                    D["vout"].wait(sc, t - 2)
                    wge(sc, "c_lkvmm", t - 1)
                sc.activation(T["vt"][:, (t % 2) * HID + s * 512:(t % 2) * HID + (s + 1) * 512],
                              src_, AF.Silu).then_inc(S["c_hvev"], 1)
            else:
                h = (s - 4) // 2
                if s in (4, 6):
                    D["gout"].wait(sc, 2 * (t - 2) + h)
                gslot = ((t % 2) * 2 + h) * DM
                sc.activation(T["gt"][:, gslot + (s % 2) * 512:gslot + (s % 2) * 512 + 512],
                              src_, AF.Silu).then_inc(S["c_hvev"], 1)
                if s in (5, 7):
                    wge(sc, "c_hvev", 8 * t + s + 1)
                    D["gout"].issue(sc, g_sp[t * 128:(t + 1) * 128, h * DM:(h + 1) * DM],
                                    T["gt"][:, gslot:gslot + DM])

        def p0_fold(s):
            sc.wait_ge(swh[s], 16)
            wge(sc, "c0t", 4 + s)
            for c in range(DC):
                sc.activation(T["wh"][:, c * 2 * HID + s * 512:c * 2 * HID + (s + 1) * 512],
                              T["wh"][:, c * 2 * HID + s * 512:c * 2 * HID + (s + 1) * 512],
                              AF.Identity,
                              scale=T["ng_fm"][:, c:c + 1]).then_inc(S["c0s"], 1)
            C0["s"] += DC
            SFOLD_CV[s] = C0["s"]

        for u in range(min(L, NT)):
            stats_sq(u)
            if u % 4 == 3:
                sqrt_batch(u // 4)
        wge(sc, "c0v", EARLY_CV[0])
        wge(sc, "c0t", 2)
        p1_n0(0)
        p1_n0(1)
        p1_ev(0)
        p1_qkev(0)
        for t in range(NT):
            p1_ev(t + 1)
            p1_n0(t + 2)
            if t >= 1:
                p1_qkev(t + 1)
            if t == 0:
                p0_fold(0)
            p1_hvev(t, 0)
            if t == 0:
                p0_fold(1)
            p1_hvev(t, 1)
            if t == 0:
                p1_qkev(1)
            u = t + L
            if u < NT:
                stats_sq(u)
                if u % 4 == 3:
                    sqrt_batch(u // 4)
            if t == 0:
                p0_fold(2)
            p1_hvev(t, 2)
            if t == 0:
                p0_fold(3)
            p1_hvev(t, 3)
            if t == 0:
                p0_fold(4)
            p1_hvev(t, 4)
            if t == 0:
                p0_fold(5)
            p1_hvev(t, 5)
            if t == 0:
                p0_fold(6)
            p1_hvev(t, 6)
            if t == 0:
                p0_fold(7)
            p1_hvev(t, 7)

        if phases < 2:
            return
        D["gout"].wait_upto(sc, 2 * NT - 1)

        def load_gate_half(k):
            u, h = divmod(k, 2)
            kp = k - 3
            if kp >= 0:
                tp, hp = divmod(kp, 2)
                wge(sc, "c_gated", 4 * tp + 2 * (hp + 1))
            D["g2"].issue(sc, T["g2"][:, (k % 3) * DM:(k % 3 + 1) * DM],
                          g_sp[u * 128:(u + 1) * 128, h * DM:(h + 1) * DM])

        for k in range(min(3, 2 * NT)):
            load_gate_half(k)
        def p2_relu(gg):
            if not (0 <= gg < NG):
                return
            wge(sc, "c_simb", gg + 1)
            sc.activation(T["sims"][:, (gg % 2) * 512:(gg % 2 + 1) * 512],
                          T["sims"][:, (gg % 2) * 512:(gg % 2 + 1) * 512],
                          AF.Relu).then_inc(S["c_relu"], 1)

        def p2_gfmev(u):
            if not (0 <= u < NT):
                return
            for b in range(4):
                wge(sc, "c_gtmm", 4 * u + b + 1)
                if b == 0:
                    wge(sc, "c_womm", 2 * u - 2)
                bank = 1024 if b % 2 == 0 else 512
                sc.copy(T["gfm"][:, (u % 2) * HID + b * 512:(u % 2) * HID + (b + 1) * 512],
                        E0[:, bank:bank + 512]).then_inc(S["c_gtev"], 1)

        def p2_ln2sq(u):
            if not (0 <= u < NT):
                return
            wge(sc, "c_pre2", u + 1)
            wge(sc, "c_sqst", NT + u)
            sc.activation(T["sq2"][:], T["pre2"][:, (u % 2) * DM:(u % 2 + 1) * DM],
                          AF.Square, accum_out=T["sumsq"][:, u:u + 1]).then_inc(S["c_sqst"], 1)

        def p2_sqrt(u):
            if not (0 <= u < NT):
                return
            wge(sc, "c_v2a", u + 1)
            sc.activation(T["s_all"][:, u:u + 1], T["rcp"][:, u:u + 1], AF.Sqrt).then_inc(S["c_s2a"], 1)

        def p2_x2f(u):
            if not (0 <= u < NT):
                return
            wge(sc, "c_v2b", u + 1)
            D["x2o"].wait(sc, u - 2)
            sc.activation(T["x2f"][:, (u % 2) * DM:(u % 2 + 1) * DM],
                          T["pre2"][:, (u % 2) * DM:(u % 2 + 1) * DM],
                          AF.Identity, bias=T["nms"][:, u:u + 1],
                          scale=T["s_all"][:, u:u + 1]).then_inc(S["c_x2f"], 1)
            wge(sc, "c_x2f", u + 1)
            D["x2o"].issue(sc, x2_sp[u * 128:(u + 1) * 128, :],
                           T["x2f"][:, (u % 2) * DM:(u % 2 + 1) * DM])

        p2_relu(0)
        for tau in range(NT):
            g = tau // 2
            _emit_xfmev(sc, tau - 3)
            p2_sqrt(tau - 2)
            if tau % 2 == 0:
                p2_relu(g + 1)
            p2_gfmev(tau - 1)
            for k in (2 * tau + 3, 2 * tau + 4):
                if k < 2 * NT:
                    load_gate_half(k)
            p2_x2f(tau - 2)
            p2_ln2sq(tau - 1)
        _emit_xfmev(sc, NT - 3)
        p2_sqrt(NT - 2)
        p2_gfmev(NT - 1)
        p2_ln2sq(NT - 1)
        p2_sqrt(NT - 1)
        p2_x2f(NT - 2)
        _emit_xfmev(sc, NT - 2)
        _emit_xfmev(sc, NT - 1)
        p2_x2f(NT - 1)

        if phases < 3:
            return
        for Ti in range(NS):
            for ci in range(32):
                wge(sc, "c_w1mm", 32 * Ti + ci + 1)
                if ci == 0:
                    wge(sc, "c_w2mm", 4 * Ti)
                sc.activation(T["h1"][:, ci * 512:(ci + 1) * 512],
                              E0[:, (ci % 4) * 512:(ci % 4 + 1) * 512],
                              AF.Relu, bias=T["b1_fm"][:, ci:ci + 1]).then_inc(S["c_h1ev"], 1)
            for tt in range(4):
                t = 4 * Ti + tt
                wge(sc, "c_pre3", t + 1)
                wge(sc, "c_sqst", 2 * NT + t)
                sc.activation(T["sq3"][:], T["pre3"][:, (t % 2) * DM:(t % 2 + 1) * DM],
                              AF.Square, accum_out=T["sumsq"][:, t:t + 1]).then_inc(S["c_sqst"], 1)
                wge(sc, "c_v3a", t + 1)
                sc.activation(T["s_all"][:, t:t + 1], T["rcp"][:, t:t + 1], AF.Sqrt).then_inc(S["c_s3a"], 1)
                wge(sc, "c_v3b", t + 1)
                D["outs"].wait(sc, t - 2)
                sc.activation(T["outst"][:, (t % 2) * DM:(t % 2 + 1) * DM],
                              T["pre3"][:, (t % 2) * DM:(t % 2 + 1) * DM],
                              AF.Identity, bias=T["nms"][:, t:t + 1],
                              scale=T["s_all"][:, t:t + 1]).then_inc(S["c_ostg"], 1)

    def _emit_xfmev(sc, tau):
        if tau < 0:
            return
        D["xfmo"].wait(sc, tau - 2)
        for b in range(2):
            wge(sc, "c_x2T", 2 * tau + b + 1)
            bank = 1536 if b == 0 else 512
            sc.copy(T["xfmt"][:, (tau % 2) * DM + b * 512:(tau % 2) * DM + (b + 1) * 512],
                    E0[:, bank:bank + 512]).then_inc(S["c_xfmev"], 1)
        wge(sc, "c_xfmev", 2 * tau + 2)
        D["xfmo"].issue(sc, xfm_sp[tau, :, :],
                        T["xfmt"][:, (tau % 2) * DM:(tau % 2 + 1) * DM])

    CV_FOLDS_DONE = [0]

    def emit_vector(ve):
        cv = 0
        vqn = [0]

        def vch(inst):
            inst.then_inc(S["vq"], 1)
            vqn[0] += 1
            return inst

        def vsync():
            if vqn[0] > 0:
                ve.wait_ge(S["vq"], vqn[0])

        ve.wait_ge(S["sPar"], 16 * NPAR)
        ve.wait_ge(S["sPar2"], 32)
        wge(ve, "c0t", 1)
        ve.tensor_copy(T["ng_fm"][:], E0[:, 1536:1536 + DC]).then_inc(S["c0v"], 1); cv += 1
        ve.tensor_copy(T["nb_fm"][:], E0[:, 1544:1544 + DC]).then_inc(S["c0v"], 1); cv += 1
        ve.tensor_copy(T["qg"][:], E0[:, 1552:1556]).then_inc(S["c0v"], 1); cv += 1
        ve.tensor_copy(T["qb"][:], E0[:, 1556:1560]).then_inc(S["c0v"], 1); cv += 1
        ve.tensor_copy(T["b1_fm"][:], E0[:, 1560:1592]).then_inc(S["c0v"], 1); cv += 1
        wge(ve, "c0v", 2)
        ve.tensor_copy(T["nb_bf"][:], T["nb_fm"][:]).then_inc(S["c0v"], 1); cv += 1
        CVD["nbbf"] = cv
        wge(ve, "c0v", 5)
        for hcol, scl in ((0, 1.0 / GSZ), (1, 1.0 / (NT * 128)), (2, 1.0), (3, 1.0)):
            ve.tensor_scalar(T["gs"][:, hcol:hcol + 1], T["qg"][:, hcol:hcol + 1], scl, None, ALU.mult).then_inc(S["c0v"], 1); cv += 1
            ve.tensor_scalar(T["bs"][:, hcol:hcol + 1], T["qb"][:, hcol:hcol + 1], scl, None, ALU.mult).then_inc(S["c0v"], 1); cv += 1
        ve.tensor_copy(T["bo_bf"][0:1, :], T["bo_f32"][0:1, :]).then_inc(S["c0v"], 1); cv += 1
        ve.tensor_copy(T["b2_bf"][0:1, :], T["b2_f32"][0:1, :]).then_inc(S["c0v"], 1); cv += 1
        wge(ve, "c0t", 2)
        ve.tensor_copy(T["r4"][:], E0[:, 1024:1028]).then_inc(S["c0v"], 1); cv += 1
        R4_CV[0] = cv
        wge(ve, "c0t", 3)
        ve.tensor_copy(T["brow_qk"][0:1, :], E0[0:1, 512:512 + QKD]).then_inc(S["c0v"], 1); cv += 1
        BROWQK_CV[0] = cv
        for c in range(DC):
            ve.tensor_scalar(T["wqk"][:, c * QKD:(c + 1) * QKD],
                             T["wqk"][:, c * QKD:(c + 1) * QKD],
                             T["ng_fm"][:, c:c + 1], None, ALU.mult).then_inc(S["c0v"], 1)
            cv += 1
        WQK_CV[0] = cv
        ve.wait_ge(S["sPar2"], 64)
        ve.tensor_tensor(T["bqk_eff"][:], T["bqk_f32"][:], T["brow_qk_t"][:], ALU.add).then_inc(S["c0v"], 1); cv += 1
        EARLY_CV[0] = cv
        ve.wait_ge(S["sRel"], 32)
        wge(ve, "c0s", 1)
        for c in range(2):
            ve.tensor_scalar(T["biasT"][:, c * 256:(c + 1) * 256],
                             bass.AP(T["hank"][:].tensor, c * 256 + 255, [[512, 128], [-1, 256]]),
                             T["s128"][:], None, ALU.mult).then_inc(S["c0v"], 1)
            cv += 1
        CV_FOLDS_DONE[0] = cv
        if phases < 1:
            return

        def p0_chain(s):
            nonlocal cv
            if s == 0:
                ve.wait_ge(S["sQK"], 16)
            wge(ve, "c0t", 4 + s)
            ve.tensor_tensor(T["bh_eff_bf"][0:1, s * 512:(s + 1) * 512],
                             T["bh_bf"][0:1, s * 512:(s + 1) * 512],
                             E0[0:1, (s % 2) * 512:(s % 2) * 512 + 512],
                             ALU.add).then_inc(S["c0v"], 1)
            cv += 1
            BHADD_CV[s] = cv

        def stats_red(u):
            D["x1"].wait(ve, u)
            ve.reduce_sum(T["sums"][:, u:u + 1], T["xt"][:, (u % XB) * DM:(u % XB + 1) * DM],
                          axis=AX).then_inc(S["c_xfv"], 1)

        def smalls_a(lo, n):
            cols = (slice(None), slice(lo, lo + n))
            vsync()
            vch(ve.tensor_scalar(T["nmu"][cols], T["sums"][cols], -1.0 / DM, None, ALU.mult))
            vsync()
            vch(ve.tensor_tensor(T["veps"][cols], T["nmu"][cols], T["nmu"][cols], ALU.mult))
            vsync()
            vch(ve.scalar_tensor_tensor(T["veps"][cols], T["sumsq"][cols], 1.0 / DM,
                                        T["veps"][cols], ALU.mult, ALU.subtract))
            vsync()
            vch(ve.tensor_scalar(T["veps"][cols], T["veps"][cols], EPS, None, ALU.add))
            vsync()
            return ve.reciprocal(T["rcp"][cols], T["veps"][cols])

        def smalls_b(lo, n):
            cols = (slice(None), slice(lo, lo + n))
            vsync()
            return ve.tensor_tensor(T["nms"][cols], T["nmu"][cols], T["s_all"][cols], ALU.mult)

        for u in range(min(L, NT)):
            stats_red(u)
            if u % 4 == 3:
                g = u // 4
                wge(ve, "c_sqst", 4 * g + 4)
                wge(ve, "c_xfv", 4 * g + 4)
                smalls_a(4 * g, 4).then_inc(S["c_v1"], 1)
                if g == 0:
                    wge(ve, "c_sq1", 1)
                    smalls_b(0, 4).then_inc(S["c_sv"], 1)
        NB1 = (NT + 3) // 4
        for t in range(NT):
            u = t + L
            if u < NT:
                stats_red(u)
                if u % 4 == 3:
                    g = u // 4
                    wge(ve, "c_sqst", 4 * g + 4)
                    wge(ve, "c_xfv", 4 * g + 4)
                    smalls_a(4 * g, 4).then_inc(S["c_v1"], 1)
            if t % 4 == 0 and t // 4 + 1 < NB1:
                gb = t // 4 + 1
                wge(ve, "c_sq1", gb + 1)
                smalls_b(4 * gb, 4).then_inc(S["c_sv"], 1)
            wge(ve, "c_qkev", t + 1)
            D["hsp"].wait_upto(ve, 3 * (t - 2) + 2)
            qsl = (t % 2) * QKD
            vch(ve.tensor_scalar(T["qq_st"][:, qsl:qsl + 128], T["qkf"][:, qsl:qsl + 128],
                             T["gs"][:, 0:1], T["bs"][:, 0:1], ALU.mult, ALU.add))
            vch(ve.tensor_scalar(T["lq_st"][:, qsl:qsl + 128], T["qkf"][:, qsl:qsl + 128],
                             T["gs"][:, 1:2], T["bs"][:, 1:2], ALU.mult, ALU.add))
            vch(ve.tensor_scalar(T["qkk_st"][:, qsl:qsl + 128], T["qkf"][:, qsl:qsl + 128],
                             T["gs"][:, 2:3], T["bs"][:, 2:3], ALU.mult, ALU.add))
            vsync()
            ve.tensor_scalar(T["lkfm"][:, qsl:qsl + 128], T["qkf"][:, qsl:qsl + 128],
                             T["gs"][:, 3:4], T["bs"][:, 3:4], ALU.mult, ALU.add).then_inc(S["c_heads"], 1)
            if t < 2:
                for s in range(4 * t, 4 * t + 4):
                    p0_chain(s)
            wge(ve, "c_lkT", t + 1)
            wge(ve, "c_lkvmm", t - 1)
            ve.tensor_copy(T["lktm"][:, qsl:qsl + 128], E1[:, 512:512 + QKD]).then_inc(S["c_lktm"], 1)
            wge(ve, "c_lkvmm", t + 1)
            wge(ve, "c_lkvadd", t)
            if t == 0:
                ve.tensor_copy(T["linkv"][:], E1[:, :]).then_inc(S["c_lkvadd"], 1)
            else:
                ve.tensor_tensor(T["linkv"][:], T["linkv"][:], E1[:, :], ALU.add).then_inc(S["c_lkvadd"], 1)

        if phases < 2:
            return
        wge(ve, "c_lkvadd", NT)
        ve.tensor_copy(T["linkv_bf"][:], T["linkv"][:]).then_inc(S["c_lkvbf"], 1)
        def p2_simb(gg):
            if not (0 <= gg < NG):
                return
            wge(ve, "c_sim", gg + 1)
            wge(ve, "c_relu", gg - 1)
            ve.tensor_tensor(T["sims"][:, (gg % 2) * 512:(gg % 2 + 1) * 512],
                             E0[:, 0:512], T["biasT"][:], ALU.add).then_inc(S["c_simb"], 1)

        def p2_attnT(gg):
            if not (0 <= gg < NG):
                return
            wge(ve, "c_relu", gg + 1)
            ve.tensor_tensor(T["attnT"][:, (gg % 2) * 512:(gg % 2 + 1) * 512],
                             T["sims"][:, (gg % 2) * 512:(gg % 2 + 1) * 512],
                             T["sims"][:, (gg % 2) * 512:(gg % 2 + 1) * 512],
                             ALU.mult).then_inc(S["c_attnT"], 1)

        def p2_gated(tau):
            for b in range(4):
                wge(ve, "c_qlin", 4 * tau + b + 1)
                h = b // 2
                D["g2"].wait(ve, 2 * tau + h)
                wge(ve, "c_gtmm", 4 * tau - 4)
                gsl = ((2 * tau + h) % 3) * DM + (b % 2) * 512
                ve.tensor_tensor(T["gtm"][:, (tau % 2) * HID + b * 512:(tau % 2) * HID + (b + 1) * 512],
                                 E1[:, b * 512:(b + 1) * 512],
                                 T["g2"][:, gsl:gsl + 512], ALU.mult).then_inc(S["c_gated"], 1)

        def p2_pre2ln(u):
            if not (0 <= u < NT):
                return
            D["x2"].wait(ve, u)
            wge(ve, "c_x2bf", u - 1)
            wge(ve, "c_x2f", u - 1)
            wge(ve, "c_womm", 2 * u + 1)
            ve.scalar_tensor_tensor(T["pre2"][:, (u % 2) * DM:(u % 2) * DM + 512],
                                    T["x2t"][:, (u % 2) * DM:(u % 2) * DM + 512],
                                    2.0, E0[:, 1536:2048], ALU.mult, ALU.add).then_inc(S["c_p2h"], 1)
            wge(ve, "c_womm", 2 * u + 2)
            wge(ve, "c_p2h", u + 1)
            vsync()
            ve.scalar_tensor_tensor(T["pre2"][:, (u % 2) * DM + 512:(u % 2) * DM + 1024],
                                    T["x2t"][:, (u % 2) * DM + 512:(u % 2) * DM + 1024],
                                    2.0, E0[:, 0:512], ALU.mult, ALU.add).then_inc(S["c_pre2"], 1)
            wge(ve, "c_pre2", u + 1)
            vch(ve.reduce_sum(T["sums"][:, u:u + 1], T["pre2"][:, (u % 2) * DM:(u % 2 + 1) * DM], axis=AX))
            wge(ve, "c_sqst", NT + u + 1)
            smalls_a(u, 1).then_inc(S["c_v2a"], 1)
            wge(ve, "c_s2a", u + 1)
            smalls_b(u, 1).then_inc(S["c_v2b"], 1)
            vsync()
            wge(ve, "c_x2T", 2 * u - 2)
            ve.tensor_scalar(T["x2bf"][:, (u % 2) * DM:(u % 2 + 1) * DM],
                             T["pre2"][:, (u % 2) * DM:(u % 2 + 1) * DM],
                             T["nmu"][:, u:u + 1], T["s_all"][:, u:u + 1],
                             ALU.add, ALU.mult).then_inc(S["c_x2bf"], 1)

        p2_simb(0)
        p2_attnT(0)
        for tau in range(NT):
            g = tau // 2
            if tau % 2 == 0:
                p2_simb(g + 1)
            else:
                p2_attnT(g + 1)
            p2_gated(tau)
            p2_pre2ln(tau - 1)
        p2_pre2ln(NT - 1)

        if phases < 3:
            return
        for Ti in range(NS):
            for tt in range(4):
                t = 4 * Ti + tt
                wge(ve, "c_w2mm", t + 1)
                D["x2r"].wait(ve, t)
                wge(ve, "c_ostg", t - 1)
                base = (tt % 2) * 1024
                ve.tensor_tensor(T["pre3"][:, (t % 2) * DM:(t % 2 + 1) * DM],
                                 E1[:, base:base + 1024],
                                 T["x2r"][:, (t % 2) * DM:(t % 2 + 1) * DM],
                                 ALU.add).then_inc(S["c_pre3"], 1)
                wge(ve, "c_pre3", t + 1)
                vch(ve.reduce_sum(T["sums"][:, t:t + 1], T["pre3"][:, (t % 2) * DM:(t % 2 + 1) * DM], axis=AX))
                wge(ve, "c_sqst", 2 * NT + t + 1)
                smalls_a(t, 1).then_inc(S["c_v3a"], 1)
                wge(ve, "c_s3a", t + 1)
                smalls_b(t, 1).then_inc(S["c_v3b"], 1)

    BROWQK_CV = [0]
    R4_CV = [0]
    WQK_CV = [0]
    EARLY_CV = [0]
    BHADD_CV = [0] * 8
    SFOLD_CV = [0] * 8
    CVD = {"nbbf": 0}

    with nc.Block() as block:
        @block.vector
        def _(ve):
            emit_vector(ve)

        @block.sync
        def _(sync):
            emit_sync(sync)

        @block.gpsimd
        def _(gp):
            emit_gpsimd(gp)

        @block.scalar
        def _(sc):
            emit_scalar(sc)

        @block.tensor
        def _(tn):
            emit_tensor(tn)

    es_out.close()
    return nc, BROWQK_CV[0], R4_CV[0]


_PROG_CACHE = {}


def _get_prog(NT):
    if NT not in _PROG_CACHE:
        _PROG_CACHE[NT] = build_program(NT)
    return _PROG_CACHE[NT]


def kernel(**inputs):
    NT = 32
    nc, _, _ = _get_prog(NT)
    ident, ones1, c128, oh, id32 = _host_consts()
    x = np.ascontiguousarray(np.asarray(inputs["L0_attn_x"], dtype=np.float32))
    B = x.shape[0]
    base = {k: np.ascontiguousarray(np.asarray(v)) for k, v in inputs.items()
            if k != "L0_attn_x"}
    base["c_ident"] = ident
    base["c_ones"] = ones1
    base["c_128"] = c128
    base["c_oh"] = oh
    base["c_id32"] = id32
    in_maps = []
    for b in range(B):
        m = dict(base)
        m["x"] = np.ascontiguousarray(x[b])
        in_maps.append(m)
    res = run_bass_kernel_spmd(nc, in_maps, core_ids=list(range(B)))
    out = np.stack([np.asarray(r["out"], dtype=np.float32) for r in res.results], axis=0)
    return out


# version 15 (final)
# speedup vs baseline: 1.0082x; 1.0082x over previous
import contextlib
import math
import numpy as np
import ml_dtypes

import concourse.bass as bass
import concourse.mybir as mybir
from concourse.bass_utils import run_bass_kernel_spmd

F32 = mybir.dt.float32
BF16 = mybir.dt.bfloat16
AF = mybir.ActivationFunctionType
ALU = mybir.AluOpType
AX = mybir.AxisListType.X

DM = 1024
HID = 2048
QKD = 128
GSZ = 256
NB = 32
MAXD = 128
EPS = 1e-5
DC = DM // 128


def _bucket_1d():
    nb = NB // 2
    max_exact = nb // 2
    out = np.zeros(511, dtype=np.int64)
    for k, d in enumerate(range(-255, 256)):
        ret = nb if d < 0 else 0
        a = abs(d)
        if a < max_exact:
            b = a
        else:
            val = np.float32(math.log(np.float32(a) / np.float32(max_exact))
                             ) if False else None
            num = np.log(np.float32(a) / np.float32(max_exact), dtype=np.float32)
            den = np.float32(math.log(MAXD / max_exact))
            b = max_exact + int(np.float32(num / den) * np.float32(nb - max_exact))
            b = min(b, nb - 1)
        out[k] = ret + b
    return out


def _host_consts():
    ident = np.eye(128, dtype=ml_dtypes.bfloat16)
    ones1 = np.ones((1, 128), dtype=ml_dtypes.bfloat16)
    c128 = np.full((128, 1), 128.0, dtype=np.float32)
    id32 = np.eye(32, dtype=np.float32)
    buck = _bucket_1d()
    oh = np.zeros((NB, 512), dtype=np.float32)
    for k in range(511):
        oh[buck[510 - k], k] = 1.0
    oh_perm = np.zeros((NB, 512), dtype=np.float32)
    for c in range(4):
        for p in range(128):
            oh_perm[:, c * 128 + p] = oh[:, 4 * p + c]
    return ident, ones1, c128, oh_perm, id32


def build_program(NT=32, dbg=False, phases=3):
    assert NT % 4 == 0
    NTOK = NT * 128
    NG = NT // 2
    NS = NT // 4
    L = 8
    XB = L

    nc = bass.Bass("TRN2", target_bir_lowering=False)

    x_h = nc.dram_tensor("x", [NTOK, DM], F32, kind="ExternalInput").ap()
    ng_h = nc.dram_tensor("L0_attn_norm_g", [DM], F32, kind="ExternalInput").ap()
    nb_h = nc.dram_tensor("L0_attn_norm_b", [DM], F32, kind="ExternalInput").ap()
    wh_h = nc.dram_tensor("L0_attn_W_h", [DM, 2 * HID], F32, kind="ExternalInput").ap()
    bh_h = nc.dram_tensor("L0_attn_b_h", [2 * HID], F32, kind="ExternalInput").ap()
    wqk_h = nc.dram_tensor("L0_attn_W_qk", [DM, QKD], F32, kind="ExternalInput").ap()
    bqk_h = nc.dram_tensor("L0_attn_b_qk", [QKD], F32, kind="ExternalInput").ap()
    qg_h = nc.dram_tensor("L0_attn_qk_gamma", [4, QKD], F32, kind="ExternalInput").ap()
    qb_h = nc.dram_tensor("L0_attn_qk_beta", [4, QKD], F32, kind="ExternalInput").ap()
    rel_h = nc.dram_tensor("L0_attn_rel_emb", [NB, 1], F32, kind="ExternalInput").ap()
    wo_h = nc.dram_tensor("L0_attn_W_o", [HID, DM], F32, kind="ExternalInput").ap()
    bo_h = nc.dram_tensor("L0_attn_b_o", [DM], F32, kind="ExternalInput").ap()
    w1_h = nc.dram_tensor("L0_ffn_W1", [DM, 4096], F32, kind="ExternalInput").ap()
    b1_h = nc.dram_tensor("L0_ffn_b1", [4096], F32, kind="ExternalInput").ap()
    w2_h = nc.dram_tensor("L0_ffn_W2", [4096, DM], F32, kind="ExternalInput").ap()
    b2_h = nc.dram_tensor("L0_ffn_b2", [DM], F32, kind="ExternalInput").ap()
    ident_h = nc.dram_tensor("c_ident", [128, 128], BF16, kind="ExternalInput").ap()
    ones_h = nc.dram_tensor("c_ones", [1, 128], BF16, kind="ExternalInput").ap()
    c128_h = nc.dram_tensor("c_128", [128, 1], F32, kind="ExternalInput").ap()
    oh_h = nc.dram_tensor("c_oh", [NB, 512], F32, kind="ExternalInput").ap()
    id32_h = nc.dram_tensor("c_id32", [32, 32], F32, kind="ExternalInput").ap()
    out_h = nc.dram_tensor("out", [NTOK, DM], F32, kind="ExternalOutput").ap()

    v_sp = nc.dram_tensor("v_sp", [NTOK, HID], BF16, kind="Internal").ap()
    g_sp = nc.dram_tensor("g_sp", [NTOK, HID], F32, kind="Internal").ap()
    qq_sp = nc.dram_tensor("qq_sp", [NT, 128, 128], BF16, kind="Internal").ap()
    qkk_sp = nc.dram_tensor("qkk_sp", [NT, 128, 128], BF16, kind="Internal").ap()
    lq_sp = nc.dram_tensor("lq_sp", [NT, 128, 128], BF16, kind="Internal").ap()
    x2_sp = nc.dram_tensor("x2_sp", [NTOK, DM], F32, kind="Internal").ap()
    xfm_sp = nc.dram_tensor("xfm_sp", [NT, 128, DM], BF16, kind="Internal").ap()
    rrev_sp = nc.dram_tensor("rrev_sp", [512], F32, kind="Internal").ap()
    w1bf_sp = nc.dram_tensor("w1bf_sp", [DM, 4096], BF16, kind="Internal").ap()
    w2bf_sp = nc.dram_tensor("w2bf_sp", [4096, DM], BF16, kind="Internal").ap()
    brow_sp = nc.dram_tensor("brow_sp", [QKD], F32, kind="Internal").ap()
    if dbg:
        dbg_h = {
            "dbg_bh": nc.dram_tensor("dbg_bh", [1, 2 * HID], BF16, kind="ExternalOutput").ap(),
            "dbg_bqk": nc.dram_tensor("dbg_bqk", [128, 1], F32, kind="ExternalOutput").ap(),
            "dbg_biasT": nc.dram_tensor("dbg_biasT", [128, 512], F32, kind="ExternalOutput").ap(),
            "dbg_linkv": nc.dram_tensor("dbg_linkv", [128, HID], F32, kind="ExternalOutput").ap(),
            "dbg_v0": nc.dram_tensor("dbg_v0", [128, HID], BF16, kind="ExternalOutput").ap(),
            "dbg_g0": nc.dram_tensor("dbg_g0", [128, HID], F32, kind="ExternalOutput").ap(),
            "dbg_x20": nc.dram_tensor("dbg_x20", [128, DM], F32, kind="ExternalOutput").ap(),
            "dbg_qq0": nc.dram_tensor("dbg_qq0", [128, 128], BF16, kind="ExternalOutput").ap(),
            "dbg_wh0": nc.dram_tensor("dbg_wh0", [128, 512], BF16, kind="ExternalOutput").ap(),
            "dbg_vL": nc.dram_tensor("dbg_vL", [128, HID], BF16, kind="ExternalOutput").ap(),
            "dbg_qqL": nc.dram_tensor("dbg_qqL", [128, 128], BF16, kind="ExternalOutput").ap(),
            "dbg_x2L": nc.dram_tensor("dbg_x2L", [128, DM], F32, kind="ExternalOutput").ap(),
            "dbg_sall": nc.dram_tensor("dbg_sall", [128, NT], F32, kind="ExternalOutput").ap(),
            "dbg_sums": nc.dram_tensor("dbg_sums", [128, NT], F32, kind="ExternalOutput").ap(),
            "dbg_sumsq": nc.dram_tensor("dbg_sumsq", [128, NT], F32, kind="ExternalOutput").ap(),
            "dbg_s128": nc.dram_tensor("dbg_s128", [128, 1], F32, kind="ExternalOutput").ap(),
            "dbg_c128": nc.dram_tensor("dbg_c128", [128, 1], F32, kind="ExternalOutput").ap(),
            "dbg_ng": nc.dram_tensor("dbg_ng", [128, DC], F32, kind="ExternalOutput").ap(),
            "dbg_id32": nc.dram_tensor("dbg_id32", [32, 32], F32, kind="ExternalOutput").ap(),
            "dbg_rrev": nc.dram_tensor("dbg_rrev", [512], F32, kind="ExternalOutput").ap(),
        }

    es_out = contextlib.ExitStack()
    sb = {}

    def alloc(stack, name, shape, dtype, side="left"):
        sb[name] = stack.enter_context(nc.sbuf_tensor(name, shape, dtype, side=side))
        return sb[name]

    es_all = es_out.enter_context(contextlib.ExitStack())
    alloc(es_all, "ident", [128, 128], BF16)
    alloc(es_all, "ones1", [1, 128], BF16)
    alloc(es_all, "c128", [128, 1], F32)
    alloc(es_all, "s128", [128, 1], F32)
    alloc(es_all, "bo_bf", [1, DM], BF16)
    alloc(es_all, "b2_bf", [1, DM], BF16)
    alloc(es_all, "b1_fm", [128, 32], F32)
    alloc(es_all, "gs", [128, 4], F32)
    alloc(es_all, "bs", [128, 4], F32)
    alloc(es_all, "bqk_eff", [128, 1], F32)
    for nm in ("sums", "sumsq", "mu", "nmu", "veps", "rcp", "s_all", "nms"):
        alloc(es_all, nm, [128, NT], F32)

    es_p012 = contextlib.ExitStack()
    alloc(es_p012, "wo", [128, 16 * DM], BF16, side="right")
    alloc(es_p012, "biasT", [128, 512], F32, side="right")
    alloc(es_p012, "linkv", [128, HID], F32, side="right")
    alloc(es_p012, "linkv_bf", [128, HID], BF16, side="right")

    es_p01 = contextlib.ExitStack()
    alloc(es_p01, "wh", [128, DC * 2 * HID], BF16)
    alloc(es_p01, "wqk", [128, DC * QKD], BF16)
    alloc(es_p01, "bh_eff_bf", [1, 2 * HID], BF16)

    es_p1a = contextlib.ExitStack()
    alloc(es_p1a, "xt", [128, XB * DM], F32)
    alloc(es_p1a, "sqscr", [128, DM], BF16)
    alloc(es_p1a, "bh_bf", [1, 2 * HID], BF16)
    alloc(es_p1a, "r4", [128, 4], F32)
    alloc(es_p1a, "hank", [128, 512], F32)
    alloc(es_p1a, "nb_bf", [128, DC], BF16)
    alloc(es_p1a, "ng_fm", [128, DC], F32)

    es_p0 = contextlib.ExitStack()
    alloc(es_p0, "oh", [NB, 512], F32)
    alloc(es_p0, "rel", [NB, 1], F32)
    alloc(es_p0, "ng_tm", [DC, 128], F32)
    alloc(es_p0, "nb_tm", [DC, 128], F32)
    alloc(es_p0, "qg_tm", [4, 128], F32)
    alloc(es_p0, "qb_tm", [4, 128], F32)
    alloc(es_p0, "b1_tm", [32, 128], F32)
    alloc(es_p0, "id32", [32, 32], F32)
    alloc(es_p0, "nb_fm", [128, DC], F32)
    alloc(es_p0, "brow_qk", [1, QKD], F32)
    alloc(es_p0, "brow_qk_t", [128, 1], F32)
    alloc(es_p0, "bqk_f32", [128, 1], F32)
    alloc(es_p0, "qg", [128, 4], F32)
    alloc(es_p0, "qb", [128, 4], F32)
    alloc(es_p0, "bo_f32", [1, DM], F32)
    alloc(es_p0, "b2_f32", [1, DM], F32)
    es_p0.close()

    es_p1 = contextlib.ExitStack()
    alloc(es_p1, "n0", [128, 3 * DM], BF16)
    alloc(es_p1, "nfm", [128, 2 * DM], BF16)
    alloc(es_p1, "qkf", [128, 2 * QKD], F32)
    alloc(es_p1, "lkfm", [128, 2 * QKD], BF16)
    alloc(es_p1, "lktm", [128, 2 * QKD], BF16)
    alloc(es_p1, "vt", [128, 2 * HID], BF16)
    alloc(es_p1, "gt", [128, 4 * DM], F32)
    alloc(es_p1, "qq_st", [128, 2 * 128], BF16)
    alloc(es_p1, "qkk_st", [128, 2 * 128], BF16)
    alloc(es_p1, "lq_st", [128, 2 * 128], BF16)
    es_p1.close()
    es_p1a.close()
    es_p01.close()

    es_p23 = es_out.enter_context(contextlib.ExitStack())
    alloc(es_p23, "w1", [128, DC * 4096], BF16)

    es_p2 = contextlib.ExitStack()
    alloc(es_p2, "qq2", [128, 3 * 256], BF16)
    alloc(es_p2, "qkk2", [128, 3 * 256], BF16)
    alloc(es_p2, "lq2", [128, 3 * 128], BF16)
    alloc(es_p2, "v2", [128, 3 * HID], BF16)
    alloc(es_p2, "g2", [128, 3 * DM], F32)
    alloc(es_p2, "x2t", [128, 2 * DM], F32)
    alloc(es_p2, "pre2", [128, 2 * DM], F32)
    alloc(es_p2, "sims", [128, 2 * 512], F32)
    alloc(es_p2, "attnT", [128, 2 * 512], BF16)
    alloc(es_p2, "gtm", [128, 2 * HID], BF16)
    alloc(es_p2, "gfm", [128, 2 * HID], BF16)
    alloc(es_p2, "x2bf", [128, 2 * DM], BF16)
    alloc(es_p2, "x2f", [128, 2 * DM], F32)
    alloc(es_p2, "xfmt", [128, 2 * DM], BF16)
    alloc(es_p2, "sq2", [128, DM], BF16)
    es_p2.close()
    es_p012.close()

    es_p3 = es_out.enter_context(contextlib.ExitStack())
    alloc(es_p3, "w2", [128, 32 * DM], BF16)
    alloc(es_p3, "xfm3", [128, 2 * 4 * DM], BF16)
    alloc(es_p3, "h1", [128, 32 * 512], BF16)
    alloc(es_p3, "x2r", [128, 2 * DM], F32)
    alloc(es_p3, "pre3", [128, 2 * DM], F32)
    alloc(es_p3, "outst", [128, 2 * DM], F32)
    alloc(es_p3, "sq3", [128, DM], BF16)

    E0 = es_out.enter_context(nc.psum_tensor("E0", [128, 2048], F32))
    E1 = es_out.enter_context(nc.psum_tensor("E1", [128, 2048], F32))

    semnames = [
        "sQK", "sWo", "sW1", "sW2", "sPar", "sPar2", "sRel", "sPC1", "sPC2",
        "c0t", "c0v", "c0s",
        "c_xfv", "c_sqst", "c_v1", "c_sq1", "c_sv", "c_n0",
        "c_T", "c_Tev", "c_qk", "c_qkev", "c_heads", "c_lkT", "c_lktm",
        "c_hvmm", "c_hvev", "c_lkvmm", "c_lkvadd",
        "c_lkvbf",
        "c_sim", "c_simb", "c_relu", "c_attnT", "c_qlin", "c_gated",
        "c_gtmm", "c_gtev", "c_womm", "c_pre2", "c_v2a", "c_s2a", "c_v2b",
        "c_x2bf", "c_x2f", "c_x2T", "c_xfmev",
        "c_w1mm", "c_h1ev", "c_w2mm", "c_pre3", "c_p2h",
        "c_v3a", "c_s3a", "c_v3b", "c_ostg", "vq",
    ]
    S = {}
    for nm in semnames:
        S[nm] = es_out.enter_context(nc.semaphore(nm))
    swh = [es_out.enter_context(nc.semaphore(f"swh{g}")) for g in range(8)]

    class DS:

        def __init__(self, name):
            self.s = [es_out.enter_context(nc.semaphore(name + "a")),
                      es_out.enter_context(nc.semaphore(name + "b"))]
            self.n = 0

        def issue(self, eng, out, in_, **kw):
            i = self.n
            if i >= 2:
                eng.wait_ge(self.s[i % 2], 16 * (i // 2))
            eng.dma_start(out=out, in_=in_, **kw).then_inc(self.s[i % 2], 16)
            self.n += 1
            return i

        def wait(self, eng, i):
            if i >= 0:
                eng.wait_ge(self.s[i % 2], 16 * (i // 2 + 1))

        def wait_upto(self, eng, i):
            self.wait(eng, i)
            self.wait(eng, i - 1)

    D = {nm: DS(nm) for nm in (
        "x1", "vout", "gout", "hsp", "qq2", "qk2", "lq2", "v2", "g2",
        "x2", "x2o", "xfmo", "xfm", "x2r", "outs")}

    def wge(eng, nm, v):
        if v > 0:
            eng.wait_ge(S[nm], v)

    T = sb

    def ap2(t, off, dims):
        return bass.AP(t[:].tensor, off, dims)

    C0 = {"t": 0, "v": 0, "s": 0}
    NPAR = 12

    def emit_sync(sync):
        sync.dma_start(out=T["ng_tm"][:], in_=ap2(ng_h, 0, [[128, DC], [1, 128]])).then_inc(S["sPar"], 16)
        sync.dma_start(out=T["nb_tm"][:], in_=ap2(nb_h, 0, [[128, DC], [1, 128]])).then_inc(S["sPar"], 16)
        sync.dma_start(out=T["bqk_f32"][:], in_=ap2(bqk_h, 0, [[1, 128], [1, 1]])).then_inc(S["sPar"], 16)
        sync.dma_start(out=T["qg_tm"][:], in_=qg_h[:, :]).then_inc(S["sPar"], 16)
        sync.wait_ge(S["sPar"], 64)
        sync.dma_start(out=T["qb_tm"][:], in_=qb_h[:, :]).then_inc(S["sPar"], 16)
        sync.dma_start(out=T["rel"][:], in_=rel_h[:, :]).then_inc(S["sPar"], 16)
        sync.dma_start(out=T["oh"][:], in_=oh_h[:, :]).then_inc(S["sPar"], 16)
        sync.dma_start(out=T["ident"][:], in_=ident_h[:, :]).then_inc(S["sPar"], 16)
        sync.wait_ge(S["sPar"], 128)
        sync.dma_start(out=T["ones1"][:], in_=ones_h[:, :]).then_inc(S["sPar"], 16)
        sync.dma_start(out=T["c128"][:], in_=c128_h[:, :]).then_inc(S["sPar"], 16)
        sync.dma_start(out=T["id32"][:], in_=id32_h[:, :]).then_inc(S["sPar"], 16)
        sync.dma_start(out=T["b1_tm"][:], in_=ap2(b1_h, 0, [[128, 32], [1, 128]])).then_inc(S["sPar"], 16)
        sync.dma_start(out=T["bo_f32"][:], in_=ap2(bo_h, 0, [[DM, 1], [1, DM]])).then_inc(S["sPar2"], 16)
        sync.dma_start(out=T["b2_f32"][:], in_=ap2(b2_h, 0, [[DM, 1], [1, DM]])).then_inc(S["sPar2"], 16)
        sync.wait_ge(S["sPar2"], 32)
        if phases >= 1:
            for t in range(min(6, XB, NT)):
                D["x1"].issue(sync, T["xt"][:, (t % XB) * DM:(t % XB + 1) * DM],
                              x_h[t * 128:(t + 1) * 128, :])
        wge(sync, "c0v", BROWQK_CV[0])
        sync.dma_start(out=brow_sp[:], in_=T["brow_qk"][0:1, :]).then_inc(S["sPar2"], 16)
        sync.wait_ge(S["sPar2"], 48)
        sync.dma_start(out=T["brow_qk_t"][:], in_=ap2(brow_sp, 0, [[1, 128], [1, 1]])).then_inc(S["sPar2"], 16)
        wge(sync, "c0v", R4_CV[0])
        sync.dma_start(out=rrev_sp.rearrange("(p c) -> p c", c=4),
                       in_=T["r4"][:]).then_inc(S["sRel"], 16)
        sync.wait_ge(S["sRel"], 16)
        sync.dma_start(out=T["hank"][:], in_=ap2(rrev_sp, 0, [[1, 128], [128, 2], [1, 256]])).then_inc(S["sRel"], 16)

        if phases < 1:
            wge(sync, "c0v", CV_FOLDS_DONE[0])
            wge(sync, "c0t", 3)
            wge(sync, "c0s", 1)
            _dbg_dump(sync)
            return
        for t in range(min(6, XB, NT), min(XB, NT)):
            D["x1"].issue(sync, T["xt"][:, (t % XB) * DM:(t % XB + 1) * DM],
                          x_h[t * 128:(t + 1) * 128, :])
        for t in range(NT):
            u = t + XB
            if u < NT:
                wge(sync, "c_n0", u - XB + 1)
                wge(sync, "c_xfv", u - XB + 1)
                D["x1"].issue(sync, T["xt"][:, (u % XB) * DM:(u % XB + 1) * DM],
                              x_h[u * 128:(u + 1) * 128, :])
            wge(sync, "c_hvev", 8 * t + 4)
            D["vout"].issue(sync, v_sp[t * 128:(t + 1) * 128, :],
                            T["vt"][:, (t % 2) * HID:(t % 2 + 1) * HID])

        if phases < 2:
            D["vout"].wait_upto(sync, NT - 1)
            D["gout"].wait_upto(sync, 2 * NT - 1)
            D["hsp"].wait_upto(sync, 3 * NT - 1)
            _dbg_dump(sync)
            return
        D["vout"].wait_upto(sync, NT - 1)
        D["gout"].wait_upto(sync, 2 * NT - 1)
        D["hsp"].wait_upto(sync, 3 * NT - 1)
        wge(sync, "c_hvmm", 8 * NT)
        wge(sync, "c_lkvmm", NT)
        wge(sync, "c_hvev", 8 * NT)
        wge(sync, "c_lkvadd", NT)

        def load_grp(g):
            D["qq2"].issue(sync, T["qq2"][:, (g % 3) * 256:(g % 3 + 1) * 256],
                           ap2(qq_sp, 2 * g * 128 * 128, [[128, 128], [128 * 128, 2], [1, 128]]))
            D["qk2"].issue(sync, T["qkk2"][:, (g % 3) * 256:(g % 3 + 1) * 256],
                           ap2(qkk_sp, 2 * g * 128 * 128, [[128, 128], [128 * 128, 2], [1, 128]]))

        def load_lq(u):
            D["lq2"].issue(sync, T["lq2"][:, (u % 3) * 128:(u % 3 + 1) * 128],
                           lq_sp[u, :, :])

        load_grp(0)
        if NT >= 1:
            load_lq(0)
            D["v2"].issue(sync, T["v2"][:, 0:HID], v_sp[0:128, :])
        if NT >= 2:
            D["v2"].issue(sync, T["v2"][:, HID:2 * HID], v_sp[128:256, :])
            D["x2"].issue(sync, T["x2t"][:, 0:DM], x_h[0:128, :])
        for g in range(1, min(3, NG)):
            load_grp(g)
        for u in range(1, min(3, NT)):
            load_lq(u)
        if NT >= 3:
            D["v2"].issue(sync, T["v2"][:, 2 * HID:3 * HID], v_sp[256:384, :])
        if NT >= 2:
            D["x2"].issue(sync, T["x2t"][:, DM:2 * DM], x_h[128:256, :])
        sync.wait_ge(S["sPC1"], 16 * 32)
        wge(sync, "c_hvmm", 8 * NT)
        wge(sync, "c_lkvmm", NT)
        for c in range(DC):
            if c == 4:
                sync.wait_ge(S["sW1"], 64)
            sync.dma_start(
                out=bass.AP(T["w1"][:].tensor, c * 4096, [[DC * 4096, 128], [1, 4096]]),
                in_=ap2(w1bf_sp, c * 128 * 4096, [[4096, 128], [1, 4096]]),
            ).then_inc(S["sW1"], 16)
        for tau in range(NT):
            g = tau // 2
            if tau % 2 == 0 and g + 3 < NG:
                wge(sync, "c_sim", g + 1)
                load_grp(g + 3)
            if tau + 3 < NT:
                wge(sync, "c_qlin", 4 * tau + 4)
                load_lq(tau + 3)
            u = tau + 3
            if u < NT:
                prev = u - 3
                last_rd = 2 * (prev // 2) + 1
                wge(sync, "c_qlin", 4 * (last_rd + 1))
                D["v2"].issue(sync, T["v2"][:, (u % 3) * HID:(u % 3 + 1) * HID],
                              v_sp[u * 128:(u + 1) * 128, :])
            u2 = tau + 2
            if u2 < NT:
                wge(sync, "c_pre2", u2 - 1)
                D["x2"].issue(sync, T["x2t"][:, (u2 % 2) * DM:(u2 % 2 + 1) * DM],
                              x_h[u2 * 128:(u2 + 1) * 128, :])

        if phases < 3:
            D["x2o"].wait_upto(sync, NT - 1)
            D["xfmo"].wait_upto(sync, NT - 1)
            _dbg_dump(sync)
            return
        D["x2o"].wait_upto(sync, NT - 1)
        D["xfmo"].wait_upto(sync, NT - 1)
        wge(sync, "c_x2T", 2 * NT)
        wge(sync, "c_xfmev", 2 * NT)
        wge(sync, "c_x2f", NT)
        wge(sync, "c_x2bf", NT)
        def load_xfm(Ti):
            for tt in range(4):
                t = 4 * Ti + tt
                D["xfm"].issue(
                    sync,
                    bass.AP(T["xfm3"][:].tensor, (Ti % 2) * 4 * DM + tt * 128,
                            [[2 * 4 * DM, 128], [512, DC], [1, 128]]),
                    xfm_sp[t, :, :])

        load_xfm(0)
        if NS > 1:
            load_xfm(1)
        sync.wait_ge(S["sPC2"], 16 * 32)
        wge(sync, "c_x2bf", NT)
        D["g2"].wait_upto(sync, 2 * NT - 1)
        D["v2"].wait_upto(sync, NT - 1)
        D["x2"].wait_upto(sync, NT - 1)
        D["qq2"].wait_upto(sync, NG - 1)
        D["qk2"].wait_upto(sync, NG - 1)
        D["lq2"].wait_upto(sync, NT - 1)
        for g in range(4):
            if g >= 1:
                sync.wait_ge(S["sW2"], 16 * g)
            sync.dma_start(
                out=bass.AP(T["w2"][:].tensor, g * 8 * DM, [[32 * DM, 128], [DM, 8], [1, DM]]),
                in_=ap2(w2bf_sp, g * 8 * 128 * DM, [[DM, 128], [128 * DM, 8], [1, DM]]),
            ).then_inc(S["sW2"], 16)
        for Ti in range(NS):
            if Ti + 2 < NS:
                wge(sync, "c_w1mm", 32 * (Ti + 1))
                load_xfm(Ti + 2)
            for tt in range(4):
                t = 4 * Ti + tt
                wge(sync, "c_pre3", t - 1)
                D["x2r"].issue(sync, T["x2r"][:, (t % 2) * DM:(t % 2 + 1) * DM],
                               x2_sp[t * 128:(t + 1) * 128, :])
            for tt in range(4):
                t = 4 * Ti + tt
                wge(sync, "c_ostg", t + 1)
                D["outs"].issue(sync, out_h[t * 128:(t + 1) * 128, :],
                                T["outst"][:, (t % 2) * DM:(t % 2 + 1) * DM])
        D["outs"].wait_upto(sync, NT - 1)
        _dbg_dump(sync)

    def _dbg_dump(sync):
        if not dbg:
            return
        dsem = S["sRel"]
        dv = [32]
        def ddump(out_ap, in_ap):
            sync.dma_start(out=out_ap, in_=in_ap).then_inc(dsem, 16)
            dv[0] += 16
            sync.wait_ge(dsem, dv[0])
        ddump(dbg_h["dbg_bh"], T["bh_eff_bf"][0:1, :])
        ddump(dbg_h["dbg_bqk"], T["bqk_eff"][:])
        ddump(dbg_h["dbg_biasT"], T["biasT"][:])
        ddump(dbg_h["dbg_linkv"], T["linkv"][:])
        ddump(dbg_h["dbg_v0"], v_sp[0:128, :])
        ddump(dbg_h["dbg_g0"], g_sp[0:128, :])
        ddump(dbg_h["dbg_x20"], x2_sp[0:128, :])
        ddump(dbg_h["dbg_qq0"], qq_sp[0, :, :])
        ddump(dbg_h["dbg_wh0"], T["wh"][:, 0:512])
        ddump(dbg_h["dbg_vL"], v_sp[(NT - 1) * 128:NT * 128, :])
        ddump(dbg_h["dbg_qqL"], qq_sp[NT - 1, :, :])
        ddump(dbg_h["dbg_x2L"], x2_sp[(NT - 1) * 128:NT * 128, :])
        ddump(dbg_h["dbg_sall"], T["s_all"][:])
        ddump(dbg_h["dbg_sums"], T["sums"][:])
        ddump(dbg_h["dbg_sumsq"], T["sumsq"][:])
        ddump(dbg_h["dbg_s128"], T["s128"][:])
        ddump(dbg_h["dbg_c128"], T["c128"][:])
        ddump(dbg_h["dbg_ng"], T["ng_fm"][:])
        ddump(dbg_h["dbg_id32"], T["id32"][:])
        ddump(dbg_h["dbg_rrev"], rrev_sp[:])

    def emit_gpsimd(gp):
        gp.dma_start(out=T["bh_bf"][:], in_=ap2(bh_h, 0, [[4096, 1], [1, 2 * HID]])).then_inc(S["sQK"], 16)
        gp.wait_ge(S["sQK"], 16)
        gp.dma_start(out=T["wqk"][:].rearrange("p (c q) -> p c q", c=DC),
                     in_=ap2(wqk_h, 0, [[QKD, 128], [128 * QKD, DC], [1, QKD]])).then_inc(S["sQK"], 16)
        for g in range(8):
            gp.dma_start(
                out=bass.AP(T["wh"][:].tensor, g * 512, [[DC * 2 * HID, 128], [2 * HID, DC], [1, 512]]),
                in_=ap2(wh_h, g * 512, [[2 * HID, 128], [128 * 2 * HID, DC], [1, 512]]),
            ).then_inc(swh[g], 16)
        for g in range(4):
            gp.dma_start(
                out=bass.AP(T["wo"][:].tensor, g * 4 * DM, [[16 * DM, 128], [DM, 4], [1, DM]]),
                in_=ap2(wo_h, g * 4 * 128 * DM, [[DM, 128], [128 * DM, 4], [1, DM]]),
            ).then_inc(S["sWo"], 16)
        if phases < 1:
            return
        npc = [0, 0]

        def precast(k):
            if k < 32:
                i = npc[0]; npc[0] += 1
                if i >= 4 and i % 4 == 0:
                    gp.wait_ge(S["sPC1"], 16 * i)
                gp.dma_start(out=ap2(w1bf_sp, k * 32 * 4096, [[4096, 32], [1, 4096]]),
                             in_=ap2(w1_h, k * 32 * 4096, [[4096, 32], [1, 4096]])
                             ).then_inc(S["sPC1"], 16)
            else:
                c = k - 32
                i = npc[1]; npc[1] += 1
                if i >= 4 and i % 4 == 0:
                    gp.wait_ge(S["sPC2"], 16 * i)
                gp.dma_start(out=ap2(w2bf_sp, c * 128 * DM, [[DM, 128], [1, DM]]),
                             in_=ap2(w2_h, c * 128 * DM, [[DM, 128], [1, DM]])
                             ).then_inc(S["sPC2"], 16)

        for t in range(NT):
            wge(gp, "c_heads", t + 1)
            sl = (t % 2) * 128
            D["hsp"].issue(gp, qq_sp[t, :, :], T["qq_st"][:, sl:sl + 128])
            D["hsp"].issue(gp, qkk_sp[t, :, :], T["qkk_st"][:, sl:sl + 128])
            D["hsp"].issue(gp, lq_sp[t, :, :], T["lq_st"][:, sl:sl + 128])
            NPT = max(NT - 4, 1)
            for k in range(min(t, NPT) * 64 // NPT, min(t + 1, NPT) * 64 // NPT):
                precast(k)

    def emit_tensor(tn):
        ID = T["ident"][:]
        tn.wait_ge(S["sPar"], 16 * NPAR)
        specs = [("ng_tm", DC, 1536), ("nb_tm", DC, 1544), ("qg_tm", 4, 1552),
                 ("qb_tm", 4, 1556), ("b1_tm", 32, 1560)]
        for i, (nm, kdim, off) in enumerate(specs):
            mm = tn.matmul(E0[:, off:off + kdim], T[nm][:, :],
                           T["id32"][0:kdim, 0:kdim],
                           start=(i == 0), stop=(i == len(specs) - 1),
                           skip_group_check=True)
        mm.then_inc(S["c0t"], 1)
        for c in range(4):
            mm = tn.matmul(E0[:, 1024 + c:1024 + c + 1],
                      T["oh"][:, c * 128:(c + 1) * 128],
                      T["rel"][:, :],
                      start=(c == 0), stop=(c == 3), skip_group_check=True)
        mm.then_inc(S["c0t"], 1)
        tn.wait_ge(S["sQK"], 32)
        wge(tn, "c0v", CVD["nbbf"])
        for c in range(DC):
            mm = tn.matmul(E0[0:1, 512:512 + QKD],
                      T["nb_bf"][:, c:c + 1],
                      T["wqk"][:, c * QKD:(c + 1) * QKD],
                      start=(c == 0), stop=(c == DC - 1))
        mm.then_inc(S["c0t"], 1)

        if phases < 1:
            return

        def p0_brow(s):
            tn.wait_ge(swh[s], 16)
            if s == 1:
                wge(tn, "c0v", BROWQK_CV[0])
            if s >= 1:
                wge(tn, "c0v", BHADD_CV[s - 1])
            wge(tn, "c_hvev", s - 1)
            pslc = E0[0:1, (s % 2) * 512:(s % 2) * 512 + 512]
            for c in range(DC):
                mm = tn.matmul(pslc,
                          T["nb_bf"][:, c:c + 1],
                          T["wh"][:, c * 2 * HID + s * 512:c * 2 * HID + (s + 1) * 512],
                          start=(c == 0), stop=(c == DC - 1))
            mm.then_inc(S["c0t"], 1)

        def p1_T(u):
            if not (0 <= u < NT):
                return
            sl3 = (u % 3) * DM
            if u == 0:
                wge(tn, "c0v", R4_CV[0])
            wge(tn, "c_n0", u + 1)
            wge(tn, "c_Tev", 2 * u - 1)
            for c in range(4):
                mm = tn.matmul(E0[:, 1024 + c * 128:1024 + (c + 1) * 128],
                               T["n0"][:, sl3 + c * 128:sl3 + (c + 1) * 128],
                               ID, start=(c == 0), stop=(c == 3), skip_group_check=True)
            mm.then_inc(S["c_T"], 1)
            wge(tn, "c_Tev", 2 * u)
            for c in range(4):
                mm = tn.matmul(E0[:, 1536 + c * 128:1536 + (c + 1) * 128],
                               T["n0"][:, sl3 + (4 + c) * 128:sl3 + (5 + c) * 128],
                               ID, start=(c == 0), stop=(c == 3), skip_group_check=True)
            mm.then_inc(S["c_T"], 1)

        def p1_qk(u):
            if not (0 <= u < NT):
                return
            sl2 = (u % 2) * DM
            if u == 0:
                wge(tn, "c0v", WQK_CV[0])
            wge(tn, "c_Tev", 2 * u + 2)
            wge(tn, "c_lkvadd", u - 1)
            for c in range(DC):
                mm = tn.matmul(E1[:, 0:QKD],
                               T["wqk"][:, c * QKD:(c + 1) * QKD],
                               T["nfm"][:, sl2 + c * 128:sl2 + (c + 1) * 128],
                               start=(c == 0), stop=(c == DC - 1))
            mm.then_inc(S["c_qk"], 1)

        def p1_hv(t, s):
            sl2 = (t % 2) * DM
            if t == 0:
                wge(tn, "c0s", SFOLD_CV[s])
                wge(tn, "c0v", BHADD_CV[s])
            wge(tn, "c_hvev", 8 * t + s - 1)
            if t == 0 and s == 0:
                wge(tn, "c_Tev", 2)
            pslc = E0[:, (s % 2) * 512:(s % 2) * 512 + 512]
            tn.matmul(pslc, T["ones1"][:], T["bh_eff_bf"][0:1, s * 512:(s + 1) * 512],
                      start=True, stop=False)
            for c in range(DC):
                mm = tn.matmul(pslc,
                               T["nfm"][:, sl2 + c * 128:sl2 + (c + 1) * 128],
                               T["wh"][:, c * 2 * HID + s * 512:c * 2 * HID + (s + 1) * 512],
                               start=False, stop=(c == DC - 1))
            mm.then_inc(S["c_hvmm"], 1)

        def p1_linkT(t):
            wge(tn, "c_heads", t + 1)
            wge(tn, "c_lkvadd", t)
            tn.matmul(E1[:, 512:512 + QKD],
                      T["lkfm"][:, (t % 2) * 128:(t % 2 + 1) * 128], ID,
                      start=True, stop=True).then_inc(S["c_lkT"], 1)

        def p1_linkv(t):
            wge(tn, "c_lktm", t + 1)
            wge(tn, "c_hvev", 8 * t + 4)
            wge(tn, "c_lkvadd", t)
            wge(tn, "c_qkev", min(t + 2, NT))
            for b in range(4):
                mm = tn.matmul(E1[:, b * 512:(b + 1) * 512],
                               T["lktm"][:, (t % 2) * 128:(t % 2 + 1) * 128],
                               T["vt"][:, (t % 2) * HID + b * 512:(t % 2) * HID + (b + 1) * 512],
                               start=True, stop=True)
            mm.then_inc(S["c_lkvmm"], 1)

        p1_T(0)
        p1_qk(0)
        for t in range(NT):
            p1_T(t + 1)
            if t == 0:
                for s in range(8):
                    p0_brow(s)
                    p1_hv(0, s)
                    if s == 1:
                        p1_qk(1)
                    elif s == 2:
                        p1_linkT(0)
                    elif s == 3:
                        p1_linkv(0)
                continue
            p1_hv(t, 0)
            p1_hv(t, 1)
            p1_qk(t + 1)
            p1_hv(t, 2)
            p1_hv(t, 3)
            p1_linkT(t)
            p1_hv(t, 4)
            p1_hv(t, 5)
            p1_linkv(t)
            p1_hv(t, 6)
            p1_hv(t, 7)

        if phases < 2:
            return
        wge(tn, "c_hvev", 8 * NT)
        wge(tn, "c_lkvadd", NT)
        tn.wait_ge(S["sWo"], 64)
        wge(tn, "c_lkvbf", 1)

        def p2_sim(gg, tau):
            if not (0 <= gg < NG):
                return
            D["qq2"].wait(tn, gg)
            D["qk2"].wait(tn, gg)
            wge(tn, "c_pre2", tau - 1)
            wge(tn, "c_attnT", gg - 1)
            for j in range(2):
                mm = tn.matmul(E0[:, j * 256:(j + 1) * 256],
                          T["qkk2"][:, (gg % 3) * 256 + j * 128:(gg % 3) * 256 + (j + 1) * 128],
                          T["qq2"][:, (gg % 3) * 256:(gg % 3 + 1) * 256],
                          start=(j == 0), stop=(j == 1), skip_group_check=True)
            mm.then_inc(S["c_sim"], 1)

        def p2_quad(tau):
            g = tau // 2
            wge(tn, "c_attnT", g + 1)
            D["v2"].wait_upto(tn, 2 * g + 1)
            D["lq2"].wait(tn, tau)
            for b in range(4):
                wge(tn, "c_gated", 4 * (tau - 1) + b + 1)
                ps = E1[:, b * 512:(b + 1) * 512]
                for j in range(2):
                    vtile = 2 * g + j
                    a0 = (g % 2) * 512 + j * 256 + (tau % 2) * 128
                    tn.matmul(ps,
                              T["attnT"][:, a0:a0 + 128],
                              T["v2"][:, (vtile % 3) * HID + b * 512:(vtile % 3) * HID + (b + 1) * 512],
                              start=(j == 0), stop=False)
                tn.matmul(ps, T["lq2"][:, (tau % 3) * 128:(tau % 3 + 1) * 128],
                          T["linkv_bf"][:, b * 512:(b + 1) * 512],
                          start=False, stop=True).then_inc(S["c_qlin"], 1)

        def p2_gatedT(u):
            wge(tn, "c_gated", 4 * u + 4)
            wge(tn, "c_gtev", 4 * (u - 1) + 4)
            wge(tn, "c_xfmev", 2 * (u - 3) + 2)
            for b in range(4):
                if b >= 2:
                    wge(tn, "c_gtev", 4 * u + b - 1)
                bank = 1024 if b % 2 == 0 else 512
                for k in range(4):
                    ch = b * 4 + k
                    mm = tn.matmul(E0[:, bank + k * 128:bank + (k + 1) * 128],
                              T["gtm"][:, (u % 2) * HID + ch * 128:(u % 2) * HID + (ch + 1) * 128],
                              ID, start=(k == 0), stop=(k == 3), skip_group_check=True)
                mm.then_inc(S["c_gtmm"], 1)

        def p2_Wo(u, tau):
            g = u // 2
            wge(tn, "c_xfmev", max(2 * (u - 3) + 1, 0))
            wge(tn, "c_pre2", u)
            for h in range(2):
                if h == 1:
                    wge(tn, "c_gtev", 4 * u + 4)
                    if tau % 2 == 0 and tau // 2 + 1 < NG:
                        wge(tn, "c_simb", tau // 2 + 2)
                    wge(tn, "c_pre2", tau - 1)
                ps = E0[:, 1536:2048] if h == 0 else E0[:, 0:512]
                tn.matmul(ps, T["ones1"][:], T["bo_bf"][0:1, h * 512:(h + 1) * 512],
                          start=True, stop=False)
                for k in range(16):
                    if h == 0 and k % 4 == 0:
                        wge(tn, "c_gtev", 4 * u + k // 4 + 1)
                    mm = tn.matmul(ps, T["gfm"][:, (u % 2) * HID + k * 128:(u % 2) * HID + (k + 1) * 128],
                              T["wo"][:, k * DM + h * 512:k * DM + (h + 1) * 512],
                              start=False, stop=(k == 15))
                mm.then_inc(S["c_womm"], 1)

        p2_sim(0, -1)
        for tau in range(NT):
            _emit_x2T(tn, tau - 4)
            p2_quad(tau)
            if tau % 2 == 0:
                p2_sim(tau // 2 + 1, tau)
            if tau >= 1:
                p2_gatedT(tau - 1)
                p2_Wo(tau - 1, tau)
        _emit_x2T(tn, NT - 4)
        p2_gatedT(NT - 1)
        p2_Wo(NT - 1, NT)
        _emit_x2T(tn, NT - 3)
        _emit_x2T(tn, NT - 2)
        _emit_x2T(tn, NT - 1)

        if phases < 3:
            return
        wge(tn, "c_xfmev", 2 * NT)
        wge(tn, "c_x2bf", NT)
        tn.wait_ge(S["sW1"], 16 * DC)
        for Ti in range(NS):
            D["xfm"].wait_upto(tn, 4 * Ti + 3)
            for ci in range(32):
                if ci % 4 == 0:
                    wge(tn, "c_h1ev", 32 * Ti + ci)
                ps = E0[:, (ci % 4) * 512:(ci % 4 + 1) * 512]
                for dc in range(DC):
                    mm = tn.matmul(ps,
                              T["w1"][:, dc * 4096 + ci * 128:dc * 4096 + (ci + 1) * 128],
                              T["xfm3"][:, (Ti % 2) * 4 * DM + dc * 512:(Ti % 2) * 4 * DM + (dc + 1) * 512],
                              start=(dc == 0), stop=(dc == DC - 1))
                mm.then_inc(S["c_w1mm"], 1)
            wge(tn, "c_h1ev", 32 * (Ti + 1))
            for tt in range(4):
                t = 4 * Ti + tt
                wge(tn, "c_pre3", t - 1)
                base = (tt % 2) * 1024
                for h in range(2):
                    ps = E1[:, base + h * 512:base + (h + 1) * 512]
                    tn.matmul(ps, T["ones1"][:], T["b2_bf"][0:1, h * 512:(h + 1) * 512],
                              start=True, stop=False)
                    for ci in range(32):
                        if Ti == 0 and tt == 0 and h == 0 and ci % 8 == 0:
                            tn.wait_ge(S["sW2"], 16 * (ci // 8 + 1))
                        mm = tn.matmul(ps,
                                  T["h1"][:, ci * 512 + tt * 128:ci * 512 + (tt + 1) * 128],
                                  T["w2"][:, ci * DM + h * 512:ci * DM + (h + 1) * 512],
                                  start=False, stop=(ci == 31))
                mm.then_inc(S["c_w2mm"], 1)

    def _emit_x2T(tn, tau):
        if tau < 0:
            return
        wge(tn, "c_x2bf", tau + 1)
        wge(tn, "c_gtev", min(4 * (tau + 1) + 4, 4 * NT))
        wge(tn, "c_p2h", min(tau + 3, NT))
        wge(tn, "c_xfmev", 2 * (tau - 1) + 2)
        for b in range(2):
            bank = 1536 if b == 0 else 512
            for k in range(4):
                c = b * 4 + k
                mm = tn.matmul(E0[:, bank + k * 128:bank + (k + 1) * 128],
                          T["x2bf"][:, (tau % 2) * DM + c * 128:(tau % 2) * DM + (c + 1) * 128],
                          T["ident"][:], start=(k == 0), stop=(k == 3), skip_group_check=True)
            mm.then_inc(S["c_x2T"], 1)

    def emit_scalar(sc):
        sc.wait_ge(S["sPar"], 16 * NPAR)
        sc.activation(T["s128"][:], T["c128"][:], AF.Sqrt).then_inc(S["c0s"], 1)
        C0["s"] += 1
        if phases < 1:
            wge(sc, "c0v", CV_FOLDS_DONE[0])
            return

        def stats_sq(u):
            D["x1"].wait(sc, u)
            wge(sc, "c_sqst", u)
            sc.activation(T["sqscr"][:],
                          T["xt"][:, (u % XB) * DM:(u % XB + 1) * DM],
                          AF.Square, accum_out=T["sumsq"][:, u:u + 1]).then_inc(S["c_sqst"], 1)

        def sqrt_batch(g):
            wge(sc, "c_v1", g + 1)
            sc.activation(T["s_all"][:, 4 * g:4 * g + 4], T["rcp"][:, 4 * g:4 * g + 4],
                          AF.Sqrt).then_inc(S["c_sq1"], 1)

        def p1_n0(u):
            if not (0 <= u < NT):
                return
            wge(sc, "c_sv", u // 4 + 1)
            wge(sc, "c_T", 2 * u - 4)
            sc.activation(T["n0"][:, (u % 3) * DM:(u % 3 + 1) * DM],
                          T["xt"][:, (u % XB) * DM:(u % XB + 1) * DM],
                          AF.Identity, bias=T["nms"][:, u:u + 1],
                          scale=T["s_all"][:, u:u + 1]).then_inc(S["c_n0"], 1)

        def p1_ev(u):
            if not (0 <= u < NT):
                return
            wge(sc, "c_T", 2 * u + 1)
            wge(sc, "c_hvmm", 8 * u - 8)
            sc.copy(T["nfm"][:, (u % 2) * DM:(u % 2) * DM + 512], E0[:, 1024:1536]).then_inc(S["c_Tev"], 1)
            wge(sc, "c_T", 2 * u + 2)
            sc.copy(T["nfm"][:, (u % 2) * DM + 512:(u % 2) * DM + 1024], E0[:, 1536:2048]).then_inc(S["c_Tev"], 1)

        def p1_qkev(u):
            if not (0 <= u < NT):
                return
            wge(sc, "c_qk", u + 1)
            wge(sc, "c_heads", u - 1)
            sc.activation(T["qkf"][:, (u % 2) * QKD:(u % 2 + 1) * QKD],
                          E1[:, 0:QKD], AF.Silu,
                          bias=T["bqk_eff"][:]).then_inc(S["c_qkev"], 1)

        def p1_hvev(t, s):
            wge(sc, "c_hvmm", 8 * t + s + 1)
            src_ = E0[:, (s % 2) * 512:(s % 2) * 512 + 512]
            if s < 4:
                if s == 0:
                    D["vout"].wait(sc, t - 2)
                    wge(sc, "c_lkvmm", t - 1)
                sc.activation(T["vt"][:, (t % 2) * HID + s * 512:(t % 2) * HID + (s + 1) * 512],
                              src_, AF.Silu).then_inc(S["c_hvev"], 1)
            else:
                h = (s - 4) // 2
                if s in (4, 6):
                    D["gout"].wait(sc, 2 * (t - 2) + h)
                gslot = ((t % 2) * 2 + h) * DM
                sc.activation(T["gt"][:, gslot + (s % 2) * 512:gslot + (s % 2) * 512 + 512],
                              src_, AF.Silu).then_inc(S["c_hvev"], 1)
                if s in (5, 7):
                    wge(sc, "c_hvev", 8 * t + s + 1)
                    D["gout"].issue(sc, g_sp[t * 128:(t + 1) * 128, h * DM:(h + 1) * DM],
                                    T["gt"][:, gslot:gslot + DM])

        def p0_fold(s):
            sc.wait_ge(swh[s], 16)
            wge(sc, "c0t", 4 + s)
            for c in range(DC):
                sc.activation(T["wh"][:, c * 2 * HID + s * 512:c * 2 * HID + (s + 1) * 512],
                              T["wh"][:, c * 2 * HID + s * 512:c * 2 * HID + (s + 1) * 512],
                              AF.Identity,
                              scale=T["ng_fm"][:, c:c + 1]).then_inc(S["c0s"], 1)
            C0["s"] += DC
            SFOLD_CV[s] = C0["s"]

        for u in range(min(L, NT)):
            stats_sq(u)
            if u % 4 == 3:
                sqrt_batch(u // 4)
        wge(sc, "c0v", EARLY_CV[0])
        wge(sc, "c0t", 2)
        p1_n0(0)
        p1_n0(1)
        p1_ev(0)
        p1_qkev(0)
        for t in range(NT):
            p1_ev(t + 1)
            p1_n0(t + 2)
            if t >= 1:
                p1_qkev(t + 1)
            if t == 0:
                p0_fold(0)
            p1_hvev(t, 0)
            if t == 0:
                p0_fold(1)
            p1_hvev(t, 1)
            if t == 0:
                p1_qkev(1)
            u = t + L
            if u < NT:
                stats_sq(u)
                if u % 4 == 3:
                    sqrt_batch(u // 4)
            if t == 0:
                p0_fold(2)
            p1_hvev(t, 2)
            if t == 0:
                p0_fold(3)
            p1_hvev(t, 3)
            if t == 0:
                p0_fold(4)
            p1_hvev(t, 4)
            if t == 0:
                p0_fold(5)
            p1_hvev(t, 5)
            if t == 0:
                p0_fold(6)
            p1_hvev(t, 6)
            if t == 0:
                p0_fold(7)
            p1_hvev(t, 7)

        if phases < 2:
            return
        D["gout"].wait_upto(sc, 2 * NT - 1)

        def load_gate_half(k):
            u, h = divmod(k, 2)
            kp = k - 3
            if kp >= 0:
                tp, hp = divmod(kp, 2)
                wge(sc, "c_gated", 4 * tp + 2 * (hp + 1))
            D["g2"].issue(sc, T["g2"][:, (k % 3) * DM:(k % 3 + 1) * DM],
                          g_sp[u * 128:(u + 1) * 128, h * DM:(h + 1) * DM])

        for k in range(min(3, 2 * NT)):
            load_gate_half(k)
        def p2_relu(gg):
            if not (0 <= gg < NG):
                return
            wge(sc, "c_simb", gg + 1)
            sc.activation(T["sims"][:, (gg % 2) * 512:(gg % 2 + 1) * 512],
                          T["sims"][:, (gg % 2) * 512:(gg % 2 + 1) * 512],
                          AF.Relu).then_inc(S["c_relu"], 1)

        def p2_gfmev(u):
            if not (0 <= u < NT):
                return
            for b in range(4):
                wge(sc, "c_gtmm", 4 * u + b + 1)
                if b == 0:
                    wge(sc, "c_womm", 2 * u - 2)
                bank = 1024 if b % 2 == 0 else 512
                sc.copy(T["gfm"][:, (u % 2) * HID + b * 512:(u % 2) * HID + (b + 1) * 512],
                        E0[:, bank:bank + 512]).then_inc(S["c_gtev"], 1)

        def p2_ln2sq(u):
            if not (0 <= u < NT):
                return
            wge(sc, "c_pre2", u + 1)
            wge(sc, "c_sqst", NT + u)
            sc.activation(T["sq2"][:], T["pre2"][:, (u % 2) * DM:(u % 2 + 1) * DM],
                          AF.Square, accum_out=T["sumsq"][:, u:u + 1]).then_inc(S["c_sqst"], 1)

        def p2_sqrt(u):
            if not (0 <= u < NT):
                return
            wge(sc, "c_v2a", u + 1)
            sc.activation(T["s_all"][:, u:u + 1], T["rcp"][:, u:u + 1], AF.Sqrt).then_inc(S["c_s2a"], 1)

        def p2_x2f(u):
            if not (0 <= u < NT):
                return
            wge(sc, "c_v2b", u + 1)
            D["x2o"].wait(sc, u - 2)
            sc.activation(T["x2f"][:, (u % 2) * DM:(u % 2 + 1) * DM],
                          T["pre2"][:, (u % 2) * DM:(u % 2 + 1) * DM],
                          AF.Identity, bias=T["nms"][:, u:u + 1],
                          scale=T["s_all"][:, u:u + 1]).then_inc(S["c_x2f"], 1)
            wge(sc, "c_x2f", u + 1)
            D["x2o"].issue(sc, x2_sp[u * 128:(u + 1) * 128, :],
                           T["x2f"][:, (u % 2) * DM:(u % 2 + 1) * DM])

        p2_relu(0)
        for tau in range(NT):
            g = tau // 2
            _emit_xfmev(sc, tau - 4)
            p2_sqrt(tau - 2)
            if tau % 2 == 0:
                p2_relu(g + 1)
            p2_gfmev(tau - 1)
            for k in (2 * tau + 3, 2 * tau + 4):
                if k < 2 * NT:
                    load_gate_half(k)
            p2_x2f(tau - 3)
            p2_ln2sq(tau - 1)
        _emit_xfmev(sc, NT - 4)
        p2_sqrt(NT - 2)
        p2_gfmev(NT - 1)
        p2_x2f(NT - 3)
        p2_ln2sq(NT - 1)
        p2_sqrt(NT - 1)
        p2_x2f(NT - 2)
        _emit_xfmev(sc, NT - 3)
        _emit_xfmev(sc, NT - 2)
        _emit_xfmev(sc, NT - 1)
        p2_x2f(NT - 1)

        if phases < 3:
            return
        for Ti in range(NS):
            for ci in range(32):
                wge(sc, "c_w1mm", 32 * Ti + ci + 1)
                if ci == 0:
                    wge(sc, "c_w2mm", 4 * Ti)
                sc.activation(T["h1"][:, ci * 512:(ci + 1) * 512],
                              E0[:, (ci % 4) * 512:(ci % 4 + 1) * 512],
                              AF.Relu, bias=T["b1_fm"][:, ci:ci + 1]).then_inc(S["c_h1ev"], 1)
            for tt in range(4):
                t = 4 * Ti + tt
                wge(sc, "c_pre3", t + 1)
                wge(sc, "c_sqst", 2 * NT + t)
                sc.activation(T["sq3"][:], T["pre3"][:, (t % 2) * DM:(t % 2 + 1) * DM],
                              AF.Square, accum_out=T["sumsq"][:, t:t + 1]).then_inc(S["c_sqst"], 1)
                wge(sc, "c_v3a", t + 1)
                sc.activation(T["s_all"][:, t:t + 1], T["rcp"][:, t:t + 1], AF.Sqrt).then_inc(S["c_s3a"], 1)
                wge(sc, "c_v3b", t + 1)
                D["outs"].wait(sc, t - 2)
                sc.activation(T["outst"][:, (t % 2) * DM:(t % 2 + 1) * DM],
                              T["pre3"][:, (t % 2) * DM:(t % 2 + 1) * DM],
                              AF.Identity, bias=T["nms"][:, t:t + 1],
                              scale=T["s_all"][:, t:t + 1]).then_inc(S["c_ostg"], 1)

    def _emit_xfmev(sc, tau):
        if tau < 0:
            return
        D["xfmo"].wait(sc, tau - 2)
        for b in range(2):
            wge(sc, "c_x2T", 2 * tau + b + 1)
            bank = 1536 if b == 0 else 512
            sc.copy(T["xfmt"][:, (tau % 2) * DM + b * 512:(tau % 2) * DM + (b + 1) * 512],
                    E0[:, bank:bank + 512]).then_inc(S["c_xfmev"], 1)
        wge(sc, "c_xfmev", 2 * tau + 2)
        D["xfmo"].issue(sc, xfm_sp[tau, :, :],
                        T["xfmt"][:, (tau % 2) * DM:(tau % 2 + 1) * DM])

    CV_FOLDS_DONE = [0]

    def emit_vector(ve):
        cv = 0
        vqn = [0]

        def vch(inst):
            inst.then_inc(S["vq"], 1)
            vqn[0] += 1
            return inst

        def vsync():
            if vqn[0] > 0:
                ve.wait_ge(S["vq"], vqn[0])

        ve.wait_ge(S["sPar"], 16 * NPAR)
        ve.wait_ge(S["sPar2"], 32)
        wge(ve, "c0t", 1)
        ve.tensor_copy(T["ng_fm"][:], E0[:, 1536:1536 + DC]).then_inc(S["c0v"], 1); cv += 1
        ve.tensor_copy(T["nb_fm"][:], E0[:, 1544:1544 + DC]).then_inc(S["c0v"], 1); cv += 1
        ve.tensor_copy(T["qg"][:], E0[:, 1552:1556]).then_inc(S["c0v"], 1); cv += 1
        ve.tensor_copy(T["qb"][:], E0[:, 1556:1560]).then_inc(S["c0v"], 1); cv += 1
        ve.tensor_copy(T["b1_fm"][:], E0[:, 1560:1592]).then_inc(S["c0v"], 1); cv += 1
        wge(ve, "c0v", 2)
        ve.tensor_copy(T["nb_bf"][:], T["nb_fm"][:]).then_inc(S["c0v"], 1); cv += 1
        CVD["nbbf"] = cv
        wge(ve, "c0v", 5)
        for hcol, scl in ((0, 1.0 / GSZ), (1, 1.0 / (NT * 128)), (2, 1.0), (3, 1.0)):
            ve.tensor_scalar(T["gs"][:, hcol:hcol + 1], T["qg"][:, hcol:hcol + 1], scl, None, ALU.mult).then_inc(S["c0v"], 1); cv += 1
            ve.tensor_scalar(T["bs"][:, hcol:hcol + 1], T["qb"][:, hcol:hcol + 1], scl, None, ALU.mult).then_inc(S["c0v"], 1); cv += 1
        ve.tensor_copy(T["bo_bf"][0:1, :], T["bo_f32"][0:1, :]).then_inc(S["c0v"], 1); cv += 1
        ve.tensor_copy(T["b2_bf"][0:1, :], T["b2_f32"][0:1, :]).then_inc(S["c0v"], 1); cv += 1
        wge(ve, "c0t", 2)
        ve.tensor_copy(T["r4"][:], E0[:, 1024:1028]).then_inc(S["c0v"], 1); cv += 1
        R4_CV[0] = cv
        wge(ve, "c0t", 3)
        ve.tensor_copy(T["brow_qk"][0:1, :], E0[0:1, 512:512 + QKD]).then_inc(S["c0v"], 1); cv += 1
        BROWQK_CV[0] = cv
        for c in range(DC):
            ve.tensor_scalar(T["wqk"][:, c * QKD:(c + 1) * QKD],
                             T["wqk"][:, c * QKD:(c + 1) * QKD],
                             T["ng_fm"][:, c:c + 1], None, ALU.mult).then_inc(S["c0v"], 1)
            cv += 1
        WQK_CV[0] = cv
        ve.wait_ge(S["sPar2"], 64)
        ve.tensor_tensor(T["bqk_eff"][:], T["bqk_f32"][:], T["brow_qk_t"][:], ALU.add).then_inc(S["c0v"], 1); cv += 1
        EARLY_CV[0] = cv
        ve.wait_ge(S["sRel"], 32)
        wge(ve, "c0s", 1)
        for c in range(2):
            ve.tensor_scalar(T["biasT"][:, c * 256:(c + 1) * 256],
                             bass.AP(T["hank"][:].tensor, c * 256 + 255, [[512, 128], [-1, 256]]),
                             T["s128"][:], None, ALU.mult).then_inc(S["c0v"], 1)
            cv += 1
        CV_FOLDS_DONE[0] = cv
        if phases < 1:
            return

        def p0_chain(s):
            nonlocal cv
            if s == 0:
                ve.wait_ge(S["sQK"], 16)
            wge(ve, "c0t", 4 + s)
            ve.tensor_tensor(T["bh_eff_bf"][0:1, s * 512:(s + 1) * 512],
                             T["bh_bf"][0:1, s * 512:(s + 1) * 512],
                             E0[0:1, (s % 2) * 512:(s % 2) * 512 + 512],
                             ALU.add).then_inc(S["c0v"], 1)
            cv += 1
            BHADD_CV[s] = cv

        def stats_red(u):
            D["x1"].wait(ve, u)
            ve.reduce_sum(T["sums"][:, u:u + 1], T["xt"][:, (u % XB) * DM:(u % XB + 1) * DM],
                          axis=AX).then_inc(S["c_xfv"], 1)

        def smalls_a(lo, n):
            cols = (slice(None), slice(lo, lo + n))
            vsync()
            vch(ve.tensor_scalar(T["nmu"][cols], T["sums"][cols], -1.0 / DM, None, ALU.mult))
            vsync()
            vch(ve.tensor_tensor(T["veps"][cols], T["nmu"][cols], T["nmu"][cols], ALU.mult))
            vsync()
            vch(ve.scalar_tensor_tensor(T["veps"][cols], T["sumsq"][cols], 1.0 / DM,
                                        T["veps"][cols], ALU.mult, ALU.subtract))
            vsync()
            vch(ve.tensor_scalar(T["veps"][cols], T["veps"][cols], EPS, None, ALU.add))
            vsync()
            return ve.reciprocal(T["rcp"][cols], T["veps"][cols])

        def smalls_b(lo, n):
            cols = (slice(None), slice(lo, lo + n))
            vsync()
            return ve.tensor_tensor(T["nms"][cols], T["nmu"][cols], T["s_all"][cols], ALU.mult)

        for u in range(min(L, NT)):
            stats_red(u)
            if u % 4 == 3:
                g = u // 4
                wge(ve, "c_sqst", 4 * g + 4)
                wge(ve, "c_xfv", 4 * g + 4)
                smalls_a(4 * g, 4).then_inc(S["c_v1"], 1)
                if g == 0:
                    wge(ve, "c_sq1", 1)
                    smalls_b(0, 4).then_inc(S["c_sv"], 1)
        NB1 = (NT + 3) // 4
        for t in range(NT):
            u = t + L
            if u < NT:
                stats_red(u)
                if u % 4 == 3:
                    g = u // 4
                    wge(ve, "c_sqst", 4 * g + 4)
                    wge(ve, "c_xfv", 4 * g + 4)
                    smalls_a(4 * g, 4).then_inc(S["c_v1"], 1)
            if t % 4 == 0 and t // 4 + 1 < NB1:
                gb = t // 4 + 1
                wge(ve, "c_sq1", gb + 1)
                smalls_b(4 * gb, 4).then_inc(S["c_sv"], 1)
            wge(ve, "c_qkev", t + 1)
            D["hsp"].wait_upto(ve, 3 * (t - 2) + 2)
            qsl = (t % 2) * QKD
            vch(ve.tensor_scalar(T["qq_st"][:, qsl:qsl + 128], T["qkf"][:, qsl:qsl + 128],
                             T["gs"][:, 0:1], T["bs"][:, 0:1], ALU.mult, ALU.add))
            vch(ve.tensor_scalar(T["lq_st"][:, qsl:qsl + 128], T["qkf"][:, qsl:qsl + 128],
                             T["gs"][:, 1:2], T["bs"][:, 1:2], ALU.mult, ALU.add))
            vch(ve.tensor_scalar(T["qkk_st"][:, qsl:qsl + 128], T["qkf"][:, qsl:qsl + 128],
                             T["gs"][:, 2:3], T["bs"][:, 2:3], ALU.mult, ALU.add))
            vsync()
            ve.tensor_scalar(T["lkfm"][:, qsl:qsl + 128], T["qkf"][:, qsl:qsl + 128],
                             T["gs"][:, 3:4], T["bs"][:, 3:4], ALU.mult, ALU.add).then_inc(S["c_heads"], 1)
            if t < 2:
                for s in range(4 * t, 4 * t + 4):
                    p0_chain(s)
            wge(ve, "c_lkT", t + 1)
            wge(ve, "c_lkvmm", t - 1)
            ve.tensor_copy(T["lktm"][:, qsl:qsl + 128], E1[:, 512:512 + QKD]).then_inc(S["c_lktm"], 1)
            wge(ve, "c_lkvmm", t + 1)
            wge(ve, "c_lkvadd", t)
            if t == 0:
                ve.tensor_copy(T["linkv"][:], E1[:, :]).then_inc(S["c_lkvadd"], 1)
            else:
                ve.tensor_tensor(T["linkv"][:], T["linkv"][:], E1[:, :], ALU.add).then_inc(S["c_lkvadd"], 1)

        if phases < 2:
            return
        wge(ve, "c_lkvadd", NT)
        ve.tensor_copy(T["linkv_bf"][:], T["linkv"][:]).then_inc(S["c_lkvbf"], 1)
        def p2_simb(gg):
            if not (0 <= gg < NG):
                return
            wge(ve, "c_sim", gg + 1)
            wge(ve, "c_relu", gg - 1)
            ve.tensor_tensor(T["sims"][:, (gg % 2) * 512:(gg % 2 + 1) * 512],
                             E0[:, 0:512], T["biasT"][:], ALU.add).then_inc(S["c_simb"], 1)

        def p2_attnT(gg):
            if not (0 <= gg < NG):
                return
            wge(ve, "c_relu", gg + 1)
            ve.tensor_tensor(T["attnT"][:, (gg % 2) * 512:(gg % 2 + 1) * 512],
                             T["sims"][:, (gg % 2) * 512:(gg % 2 + 1) * 512],
                             T["sims"][:, (gg % 2) * 512:(gg % 2 + 1) * 512],
                             ALU.mult).then_inc(S["c_attnT"], 1)

        def p2_gated(tau):
            for b in range(4):
                wge(ve, "c_qlin", 4 * tau + b + 1)
                h = b // 2
                D["g2"].wait(ve, 2 * tau + h)
                wge(ve, "c_gtmm", 4 * tau - 4)
                gsl = ((2 * tau + h) % 3) * DM + (b % 2) * 512
                ve.tensor_tensor(T["gtm"][:, (tau % 2) * HID + b * 512:(tau % 2) * HID + (b + 1) * 512],
                                 E1[:, b * 512:(b + 1) * 512],
                                 T["g2"][:, gsl:gsl + 512], ALU.mult).then_inc(S["c_gated"], 1)

        def p2_pre2ln(u):
            if not (0 <= u < NT):
                return
            D["x2"].wait(ve, u)
            wge(ve, "c_x2bf", u - 1)
            wge(ve, "c_x2f", u - 1)
            wge(ve, "c_womm", 2 * u + 1)
            ve.scalar_tensor_tensor(T["pre2"][:, (u % 2) * DM:(u % 2) * DM + 512],
                                    T["x2t"][:, (u % 2) * DM:(u % 2) * DM + 512],
                                    2.0, E0[:, 1536:2048], ALU.mult, ALU.add).then_inc(S["c_p2h"], 1)
            wge(ve, "c_womm", 2 * u + 2)
            wge(ve, "c_p2h", u + 1)
            vsync()
            ve.scalar_tensor_tensor(T["pre2"][:, (u % 2) * DM + 512:(u % 2) * DM + 1024],
                                    T["x2t"][:, (u % 2) * DM + 512:(u % 2) * DM + 1024],
                                    2.0, E0[:, 0:512], ALU.mult, ALU.add).then_inc(S["c_pre2"], 1)
            wge(ve, "c_pre2", u + 1)
            vch(ve.reduce_sum(T["sums"][:, u:u + 1], T["pre2"][:, (u % 2) * DM:(u % 2 + 1) * DM], axis=AX))
            wge(ve, "c_sqst", NT + u + 1)
            smalls_a(u, 1).then_inc(S["c_v2a"], 1)

        def p2_pre2ln_b(u):
            if not (0 <= u < NT):
                return
            wge(ve, "c_s2a", u + 1)
            smalls_b(u, 1).then_inc(S["c_v2b"], 1)
            vsync()
            wge(ve, "c_x2T", 2 * u - 2)
            ve.tensor_scalar(T["x2bf"][:, (u % 2) * DM:(u % 2 + 1) * DM],
                             T["pre2"][:, (u % 2) * DM:(u % 2 + 1) * DM],
                             T["nmu"][:, u:u + 1], T["s_all"][:, u:u + 1],
                             ALU.add, ALU.mult).then_inc(S["c_x2bf"], 1)

        p2_simb(0)
        p2_attnT(0)
        for tau in range(NT):
            g = tau // 2
            if tau % 2 == 0:
                p2_simb(g + 1)
            else:
                p2_attnT(g + 1)
            p2_gated(tau)
            p2_pre2ln(tau - 1)
            p2_pre2ln_b(tau - 2)
        p2_pre2ln(NT - 1)
        p2_pre2ln_b(NT - 2)
        p2_pre2ln_b(NT - 1)

        if phases < 3:
            return
        for Ti in range(NS):
            for tt in range(4):
                t = 4 * Ti + tt
                wge(ve, "c_w2mm", t + 1)
                D["x2r"].wait(ve, t)
                wge(ve, "c_ostg", t - 1)
                base = (tt % 2) * 1024
                ve.tensor_tensor(T["pre3"][:, (t % 2) * DM:(t % 2 + 1) * DM],
                                 E1[:, base:base + 1024],
                                 T["x2r"][:, (t % 2) * DM:(t % 2 + 1) * DM],
                                 ALU.add).then_inc(S["c_pre3"], 1)
                wge(ve, "c_pre3", t + 1)
                vch(ve.reduce_sum(T["sums"][:, t:t + 1], T["pre3"][:, (t % 2) * DM:(t % 2 + 1) * DM], axis=AX))
                wge(ve, "c_sqst", 2 * NT + t + 1)
                smalls_a(t, 1).then_inc(S["c_v3a"], 1)
                wge(ve, "c_s3a", t + 1)
                smalls_b(t, 1).then_inc(S["c_v3b"], 1)

    BROWQK_CV = [0]
    R4_CV = [0]
    WQK_CV = [0]
    EARLY_CV = [0]
    BHADD_CV = [0] * 8
    SFOLD_CV = [0] * 8
    CVD = {"nbbf": 0}

    with nc.Block() as block:
        @block.vector
        def _(ve):
            emit_vector(ve)

        @block.sync
        def _(sync):
            emit_sync(sync)

        @block.gpsimd
        def _(gp):
            emit_gpsimd(gp)

        @block.scalar
        def _(sc):
            emit_scalar(sc)

        @block.tensor
        def _(tn):
            emit_tensor(tn)

    es_out.close()
    return nc, BROWQK_CV[0], R4_CV[0]


_PROG_CACHE = {}


def _get_prog(NT):
    if NT not in _PROG_CACHE:
        _PROG_CACHE[NT] = build_program(NT)
    return _PROG_CACHE[NT]


def kernel(**inputs):
    NT = 32
    nc, _, _ = _get_prog(NT)
    ident, ones1, c128, oh, id32 = _host_consts()
    x = np.ascontiguousarray(np.asarray(inputs["L0_attn_x"], dtype=np.float32))
    B = x.shape[0]
    base = {k: np.ascontiguousarray(np.asarray(v)) for k, v in inputs.items()
            if k != "L0_attn_x"}
    base["c_ident"] = ident
    base["c_ones"] = ones1
    base["c_128"] = c128
    base["c_oh"] = oh
    base["c_id32"] = id32
    in_maps = []
    for b in range(B):
        m = dict(base)
        m["x"] = np.ascontiguousarray(x[b])
        in_maps.append(m)
    res = run_bass_kernel_spmd(nc, in_maps, core_ids=list(range(B)))
    out = np.stack([np.asarray(r["out"], dtype=np.float32) for r in res.results], axis=0)
    return out
